# version 37 (final)
# speedup vs baseline: 1.0129x; 1.0022x over previous
import math
import numpy as np
import ml_dtypes
import concourse.bass as bass
import concourse.mybir as mybir
from concourse.bass_utils import run_bass_kernel_spmd

F32 = mybir.dt.float32
BF16 = mybir.dt.bfloat16
U8 = mybir.dt.uint8
AF = mybir.ActivationFunctionType
ALU = mybir.AluOpType
AX = mybir.AxisListType

BF = ml_dtypes.bfloat16

D = 1024
NH_R = 16
DH_R = 64
NH_F = 8
DH_F = 64
INNER = 512
HID = 2730
NJ = 22
LAST_J = HID - 21 * 128
WINDOW = 512
QK_SCALE = 8.0
EPS = 1e-5
NCORES = 8


def build_consts(T):
    NT = T // 128
    c = {}
    c["identity"] = np.eye(128, dtype=np.float32).astype(BF)
    c["ones_row"] = np.ones((1, 128), dtype=BF)
    c["ones_row_f32"] = np.ones((1, 128), dtype=np.float32)
    c["ones_mat"] = np.ones((128, 128), dtype=BF)
    hs = np.zeros((128, 2), dtype=np.float32)
    hs[:64, 0] = 1.0
    hs[64:, 1] = 1.0
    c["halfsel"] = hs

    gam = 1.0 - 2.0 ** (-5.0 - np.arange(NH_R, dtype=np.float64))
    p = np.arange(128, dtype=np.float64)[:, None]
    c["gq_tbl"] = (gam[None, :] ** p / 8.0).astype(np.float32)
    c["gk_tbl"] = (gam[None, :] ** (-p)).astype(np.float32)
    blk = np.arange(8, dtype=np.int64)
    half = (np.arange(128) >= 64).astype(np.int64)
    headmap = 2 * blk[None, :] + half[:, None]
    gh = gam[headmap]
    c["gC1_tbl"] = (gh ** 127).astype(np.float32)
    c["gC_tbl"] = (gh ** 128).astype(np.float32)
    c["g1_tbl"] = gh.astype(np.float32)
    tp = np.arange(128)
    c["mask_keep"] = (tp[:, None] <= tp[None, :]).astype(np.float32)

    NEG = -1e30
    c["amask_lo"] = np.where(tp[None, :] < tp[:, None], NEG, 0.0).astype(np.float32)
    c["amask_hi"] = np.where(tp[None, :] > tp[:, None], NEG, 0.0).astype(np.float32)

    half_idx = np.arange(0, DH_F, 2, dtype=np.float64)
    inv_freq = 1.0 / 10000.0 ** (half_idx / DH_F)
    t = np.arange(T, dtype=np.float64)
    freqs = t[:, None] * inv_freq[None, :]
    freqs = np.concatenate([freqs, freqs], axis=-1)
    base = (half_idx + 0.4 * DH_F) / (1.4 * DH_F)
    power = (t - T // 2) / float(WINDOW // 2)
    scale = base[None, :] ** power[:, None]
    scale = np.concatenate([scale, scale], axis=-1)
    cos = np.cos(freqs)
    sin = np.sin(freqs)
    sign = np.where(np.arange(DH_F) < 32, -1.0, 1.0)[None, :]
    cq = cos * scale
    sq = sin * scale * sign
    ck = cos / scale
    sk = sin / scale * sign
    for nm, tab in (("rot_cq", cq), ("rot_sq", sq), ("rot_ck", ck), ("rot_sk", sk)):
        c[nm] = np.tile(tab.astype(BF), (1, NH_F))
    return c


def layout_inputs(inputs, T):
    d = {}
    d["gnw_fm"] = np.ascontiguousarray(inputs["L0_attn_gn_weight"].reshape(8, 128).T)
    d["gnb_fm"] = np.ascontiguousarray(inputs["L0_attn_gn_bias"].reshape(8, 128).T)
    d["lg_fm"] = np.ascontiguousarray(inputs["L0_ffn_ln_g"].reshape(2, 8, 128).transpose(0, 2, 1))
    d["lb_fm"] = np.ascontiguousarray(inputs["L0_ffn_ln_b"].reshape(2, 8, 128).transpose(0, 2, 1))
    d["b2_fm"] = np.ascontiguousarray(inputs["L0_ffn_b2"].reshape(2, 8, 128).transpose(0, 2, 1))
    b1 = inputs["L0_ffn_b1"]
    b1a = np.zeros((2, 128, NJ), dtype=np.float32)
    b1g = np.zeros((2, 128, NJ), dtype=np.float32)
    for j in range(NJ):
        n = 128 if j < NJ - 1 else LAST_J
        b1a[:, :n, j] = b1[:, j * 128:j * 128 + n]
        b1g[:, :n, j] = b1[:, HID + j * 128:HID + j * 128 + n]
    d["b1a"] = b1a
    d["b1g"] = b1g
    qs = inputs["L0_ffn_q_scale"]
    ks = inputs["L0_ffn_k_scale"]
    d["qs_rep"] = np.ascontiguousarray(
        np.broadcast_to(np.tile(qs, (1, NH_F))[:, None, :], (2, 128, 512))).astype(np.float32)
    d["ks_rep"] = np.ascontiguousarray(
        np.broadcast_to(np.tile(ks, (1, NH_F))[:, None, :], (2, 128, 512))).astype(np.float32)
    return d


ENGS = ("tensor", "vector", "scalar", "gpsimd", "sync")


class Prog:

    def __init__(self, nc):
        self.nc = nc
        self.queues = {e: [] for e in ENGS}
        self.sems = {}
        self.cnt = {}
        self.waited = {e: {} for e in ENGS}
        self._ctxs = []
        for key in ("pe", "dve", "act", "pool", "sp", "dsp", "dact", "dpool"):
            cm = nc.semaphore(f"s_{key}")
            self.sems[key] = cm.__enter__()
            self._ctxs.append(cm)
            self.cnt[key] = 0

    ENG2SEM = {"tensor": "pe", "vector": "dve", "scalar": "act", "gpsimd": "pool", "sync": "sp"}
    DMAQ2SEM = {"sync": "dsp", "scalar": "dact", "gpsimd": "dpool"}

    def _waits(self, eng, deps):
        out = []
        w = self.waited[eng]
        for (sk, val) in deps:
            if val is None:
                continue
            if w.get(sk, 0) < val:
                out.append((self.sems[sk], val))
                w[sk] = val
        return out

    def op(self, eng, fn, deps=(), inc=True):
        waits = self._waits(eng, deps)
        semkey = self.ENG2SEM[eng]
        tok = None
        if inc:
            self.cnt[semkey] += 1
            tok = (semkey, self.cnt[semkey])
            if eng == "tensor":
                self.waited[eng][semkey] = self.cnt[semkey]
        sem = self.sems[semkey]

        def run(e, waits=waits, fn=fn, inc=inc, sem=sem):
            for (s, v) in waits:
                e.wait_ge(s, v)
            inst = fn(e)
            if inc:
                assert inst is not None, "inc=True op returned no instruction"
                inst.then_inc(sem, 1)
        self.queues[eng].append(run)
        return tok

    def dma(self, queue, out_ap, in_ap, deps=(), slow=False):
        eng = queue
        waits = self._waits(eng, deps)
        semkey = self.DMAQ2SEM[queue]
        self.cnt[semkey] += 16
        tok = (semkey, self.cnt[semkey])
        sem = self.sems[semkey]

        def run(e, waits=waits, out_ap=out_ap, in_ap=in_ap, sem=sem, slow=slow):
            for (s, v) in waits:
                e.wait_ge(s, v)
            if slow:
                e.dma_start(out_ap, in_ap, allow_slow_non_contiguous=True).then_inc(sem, 16)
            else:
                e.dma_start(out_ap, in_ap).then_inc(sem, 16)
        self.queues[eng].append(run)
        return tok

    def raw(self, eng, fn):
        self.queues[eng].append(fn)

    def barrier(self):
        targets = [(k, v) for k, v in self.cnt.items() if v > 0]
        for eng in ENGS:
            waits = self._waits(eng, targets)
            if not waits:
                continue

            def run(e, waits=waits):
                for (sem, v) in waits:
                    e.wait_ge(sem, v)
            self.queues[eng].append(run)

    def finish(self, block):
        for eng in ENGS:
            q = self.queues[eng]
            getattr(block, eng)(lambda e, q=q: [f(e) for f in q])


class Arena:

    def __init__(self, nc, nbytes, name="arena"):
        self.nbytes = nbytes
        self.cm = nc.sbuf_tensor(name, [128, nbytes], U8)
        self.t = self.cm.__enter__()
        self.cur = 0
        self.marks = []
        self.live = {}

    def alloc(self, name, nbytes):
        nbytes = (nbytes + 63) & ~63
        off = self.cur
        assert off + nbytes <= self.nbytes, (
            f"arena overflow: {name} needs {nbytes} at {off}, cap {self.nbytes}")
        self.cur += nbytes
        self.live[name] = (off, nbytes)
        return off

    def push(self):
        self.marks.append(self.cur)

    def pop(self):
        self.cur = self.marks.pop()

    def view(self, off, dtype, *shape):
        esz = {F32: 4, BF16: 2}[dtype]
        n = 1
        for s in shape:
            n *= s
        ap = self.t.ap()[:, off:off + n * esz].bitcast(dtype)
        if len(shape) > 1:
            kw = {}
            names = []
            for i, s in enumerate(shape):
                names.append(f"d{i}")
                kw[f"d{i}"] = s
            ap = ap.rearrange(f"p ({' '.join(names)}) -> p {' '.join(names)}", **kw)
        return ap


class Buf:

    def __init__(self, arena, name, dtype, shape):
        esz = {F32: 4, BF16: 2}[dtype]
        n = 1
        for s in shape:
            n *= s
        self.arena = arena
        self.off = arena.alloc(name, n * esz)
        self.dtype = dtype
        self.shape = shape

    def ap(self, *shape):
        return self.arena.view(self.off, self.dtype, *(shape if shape else self.shape))


class Bank:

    def __init__(self, nc, i):
        self.cm = nc.psum_tensor(f"pb{i}", [128, 512], F32)
        self.t = self.cm.__enter__()
        self.wtok = None
        self.rtok = None

    def f32(self, lo=0, hi=512):
        return self.t.ap()[:, lo:hi]

    def f32v(self, lo, *shape):
        ap = self.t.ap()
        n = 1
        for s in shape:
            n *= s
        ap = ap[:, lo:lo + n]
        if len(shape) > 1:
            kw = {f"d{i}": s for i, s in enumerate(shape)}
            ap = ap.rearrange(f"p ({' '.join(kw)}) -> p {' '.join(kw)}", **kw)
        return ap

    def bf16(self, lo=0, hi=512):
        return self.t.ap()[:, lo:hi].bitcast(BF16)


def build_core(T=2048, nseq=2, dbg=(), stop_phase=99, retpart='full', paranoid=0):
    NT = T // 128
    nc = bass.Bass("TRN2", target_bir_lowering=False, detect_race_conditions=False)

    def din(name, shape, dt=F32):
        return nc.dram_tensor(name, list(shape), dt, kind="ExternalInput").ap()

    x_in = din("x", [nseq, T, D])
    y_out = nc.dram_tensor("y", [nseq, T, D], F32, kind="ExternalOutput").ap()

    wq_d = din("L0_attn_W_q", [D, D])
    wk_d = din("L0_attn_W_k", [D, D])
    wv_d = din("L0_attn_W_v", [D, D])
    wo_d = din("L0_attn_W_o", [D, D])
    wqkv_d = din("L0_ffn_wqkv", [2, D, 3 * INNER])
    fwo_d = din("L0_ffn_wo", [2, INNER, D])
    w1_d = din("L0_ffn_w1", [2, D, 2 * HID])
    w2_d = din("L0_ffn_w2", [2, HID, D])

    identity_d = din("identity", [128, 128], BF16)
    ones_mat_d = din("ones_mat", [128, 128], BF16)
    ones_row_f32_d = din("ones_row_f32", [1, 128])
    gq_d = din("gq_tbl", [128, 16])
    gk_d = din("gk_tbl", [128, 16])
    gC1_d = din("gC1_tbl", [128, 8])
    gC_d = din("gC_tbl", [128, 8])
    g1_d = din("g1_tbl", [128, 8])
    maskk_d = din("mask_keep", [128, 128])
    amlo_d = din("amask_lo", [128, 128])
    amhi_d = din("amask_hi", [128, 128])
    rot_d = {nm: din(nm, [T, 512], BF16) for nm in ("rot_cq", "rot_sq", "rot_ck", "rot_sk")}
    gnw_d = din("gnw_fm", [128, 8])
    gnb_d = din("gnb_fm", [128, 8])
    lg_d = din("lg_fm", [2, 128, 8])
    lb_d = din("lb_fm", [2, 128, 8])
    b2_d = din("b2_fm", [2, 128, 8])
    b1a_d = din("b1a", [2, 128, NJ])
    b1g_d = din("b1g", [2, 128, NJ])
    qs_d = din("qs_rep", [2, 128, 512])
    ks_d = din("ks_rep", [2, 128, 512])
    halfsel_d = din("halfsel", [128, 2])

    xo_spill_d = nc.dram_tensor("xo_spill", [T, D], F32, kind="Internal").ap()

    wattn_scr = {nm: nc.dram_tensor(f"w{nm}_scr", [128, 8, 1024], BF16,
                                    kind="Internal").ap()
                 for nm in ("q", "k", "v", "o")}
    wqkv_scr = nc.dram_tensor("wqkv_scr", [2, 128, 8, 3 * INNER], BF16,
                              kind="Internal").ap()
    fwo_scr = nc.dram_tensor("fwo_scr", [2, 128, 4, D], BF16, kind="Internal").ap()
    w1_scr = nc.dram_tensor("w1_scr", [2, 128, 8, 2 * HID], BF16,
                            kind="Internal").ap()
    w2_scr = nc.dram_tensor("w2_scr", [2, 128, NJ, D], BF16, kind="Internal").ap()
    wtT_scr = {nm: nc.dram_tensor(f"wtT_{nm}_scr", [128, 8, 1024], BF16,
                                  kind="Internal").ap()
               for nm in ("q", "k", "v", "o")}

    dbg_shapes = dict(
        dbg_lnxfm=[128, 8 * (T + 4)], dbg_ytm=[128, NT * 1024],
        dbg_qtm_r=[128, 1024], dbg_ktm_r=[128, 1024], dbg_vtm_r=[128, 1024],
        dbg_wqT=[128, 8 * 1024], dbg_wkT=[128, 8 * 1024],
        dbg_wqT_early=[128, 8 * 1024], dbg_idn=[128, 128],
        dbg_qraw0=[128, 512], dbg_rstd0=[128, 8], dbg_q10=[128, 512],
        dbg_qfm_early=[128, 4 * T], dbg_qfm_t0a=[128, 4 * 128], dbg_qfm_t0b=[128, 4 * 128],
        dbg_xob_ld=[128, 1024], dbg_xob_o1=[128, 1024], dbg_f2rstd=[128, 2],
        dbg_smalls=[128, 64], dbg_bounce=[128, 1024],
        dbg_t10=[128, 512], dbg_t20=[128, 512], dbg_qft0=[128, 512],
        dbg_asb=[128, 16 * 128], dbg_qfmc=[128, 1024], dbg_kfmc=[128, 1024],
        dbg_xattn=[nseq, T, D], dbg_qfm=[128, 4 * T], dbg_kfm=[128, 4 * T],
        dbg_vtm=[128, NT * 512], dbg_ofm=[128, 4 * T], dbg_xs1=[T, D],
        dbg_h2=[128, 8 * T], dbg_xsfin=[T, D], dbg_ynfm=[128, 8 * T],
    )
    dbg_out = {}
    for name in dbg:
        dbg_out[name] = nc.dram_tensor(
            name, dbg_shapes[name], F32, kind="ExternalOutput").ap()

    ar = Arena(nc, 206 * 1024)
    banks = [Bank(nc, i) for i in range(8)]
    p = Prog(nc)

    FMW = T + 4
    XO = Buf(ar, "XO", F32, (NT, 1024))
    FMB = Buf(ar, "FM", BF16, (8, FMW))
    TM16 = Buf(ar, "TM16", BF16, (max(NT, 16), 1024))
    RD = Buf(ar, "RD", BF16, (24, 1024))
    SCR = Buf(ar, "SCR", F32, (6 * 1024,))
    CST = Buf(ar, "CST", F32, (1024,))

    co = {}
    _c = [0]

    def calloc(name, n):
        co[name] = _c[0]
        _c[0] += n
    for nm, n in (("gq", 16), ("gk", 16), ("gC1", 8), ("gC", 8), ("g1", 8),
                  ("maskk", 128), ("amlo", 128), ("amhi", 128),
                  ("halfsel", 2), ("gnw", 8), ("gnb", 8),
                  ("identity", 64), ("ones_mat", 64), ("ones_row_f32", 128),
                  ("lg", 8), ("lb", 8), ("b2", 8), ("b1a", NJ), ("b1g", NJ),
                  ("epscol", 1), ("ln8col", 1), ("onecol", 1), ("n2eps", 1),
                  ("lnn", 1), ("tinyc", 1), ("smalls", 64)):
        calloc(nm, n)
    assert _c[0] <= 1024, _c[0]

    def cv(name, dtype, *shape):
        return ar.view(CST.off + co[name] * 4, dtype, *shape)

    def sv(byte_off, dtype, *shape):
        return ar.view(SCR.off + byte_off, dtype, *shape)

    def rdv(byte_off, dtype, *shape):
        return ar.view(RD.off + byte_off, dtype, *shape)

    def tmv(byte_off, dtype, *shape):
        return ar.view(TM16.off + byte_off, dtype, *shape)

    def sm(i):
        return cv("smalls", F32, 64)[:, i:i + 1]

    IDN = cv("identity", BF16, 128)
    ONES_F32 = cv("ones_row_f32", F32, 128)[0:1, :]
    EPSC = cv("epscol", F32, 1)
    N2EPS = cv("n2eps", F32, 1)
    TINYC = cv("tinyc", F32, 1)
    LNN = cv("lnn", F32, 1)
    LN8C = cv("ln8col", F32, 1)
    ONEC = cv("onecol", F32, 1)

    p.op("vector", lambda e: e.memset(cv("epscol", F32, 1), EPS))
    p.op("vector", lambda e: e.memset(cv("n2eps", F32, 1), 1024.0 * 1024.0 * EPS))
    p.op("vector", lambda e: e.memset(cv("tinyc", F32, 1), 1e-24))
    p.op("vector", lambda e: e.memset(cv("lnn", F32, 1), float(math.log(1024.0))))
    p.op("vector", lambda e: e.memset(cv("onecol", F32, 1), 1.0))
    p.op("vector", lambda e: e.memset(cv("ln8col", F32, 1), float(math.log(QK_SCALE))))
    ctoks = []
    for name, dram in (("gq", gq_d), ("gk", gk_d), ("gC1", gC1_d), ("gC", gC_d),
                       ("g1", g1_d), ("maskk", maskk_d), ("amlo", amlo_d),
                       ("amhi", amhi_d), ("halfsel", halfsel_d),
                       ("gnw", gnw_d), ("gnb", gnb_d),
                       ("ones_row_f32", ones_row_f32_d)):
        dst = cv(name, F32, *dram.shape[1:])
        if dram.shape[0] != 128:
            dst = dst[0:dram.shape[0], :]
        ctoks.append(p.dma("sync", dst, dram))
    ctoks.append(p.dma("sync", IDN, identity_d))
    ctoks.append(p.dma("sync", cv("ones_mat", BF16, 128), ones_mat_d))
    ONES_MAT = cv("ones_mat", BF16, 128)
    ctok = ctoks[-1]

    def mm(bank, lo, hi, lhsT, rhs, start, stop, deps=(), inc=None, tile_position=None,
           sg=False):
        d = list(deps)
        if start and bank.rtok is not None:
            d.append(bank.rtok)
        do_inc = stop if inc is None else inc
        tok = p.op("tensor",
                   lambda e, b=bank, lo=lo, hi=hi, l=lhsT, r=rhs, s=start, t=stop,
                   tp=tile_position, sg=sg:
                   e.matmul(b.f32(lo, hi), l, r, start=s, stop=t, tile_position=tp,
                            skip_group_check=sg),
                   deps=d, inc=do_inc)
        if do_inc:
            bank.wtok = tok
        return tok

    def mm_ap(bank, out_ap, lhsT, rhs, start, stop, deps=(), inc=None,
              tile_position=None, skip_group_check=False):
        d = list(deps)
        if start and bank.rtok is not None:
            d.append(bank.rtok)
        do_inc = stop if inc is None else inc
        tok = p.op("tensor",
                   lambda e, o=out_ap, l=lhsT, r=rhs, s=start, t=stop, tp=tile_position,
                   sg=skip_group_check:
                   e.matmul(o, l, r, start=s, stop=t, tile_position=tp,
                            skip_group_check=sg),
                   deps=d, inc=do_inc)
        if do_inc:
            bank.wtok = tok
        return tok

    def transpose(bank, src_ap, deps=()):
        d = list(deps) + [ctok]
        if bank.rtok is not None:
            d.append(bank.rtok)
        tok = p.op("tensor",
                   lambda e, b=bank, s=src_ap:
                   e.transpose(b.t.ap()[:, 0:64].bitcast(BF16), s, IDN),
                   deps=d, inc=True)
        bank.wtok = tok
        return tok

    def evac(bank, dst_ap, deps=(), src_lo=0, src_n=128, bf=True):
        d = list(deps) + [bank.wtok]
        tok = p.op("vector",
                   lambda e, b=bank, dst=dst_ap, lo=src_lo, n=src_n:
                   e.tensor_copy(dst, b.t.ap()[:, lo // 2:(lo + n) // 2].bitcast(BF16)),
                   deps=d, inc=True)
        bank.rtok = tok
        return tok

    def vop(fn, deps=(), inc=True):
        return p.op("vector", fn, deps=deps, inc=inc)

    def dbg_dump(name, src, deps):
        tok = p.dma("gpsimd", dbg_out[name], src, deps=deps)
        for eng in ENGS:
            p.op(eng, lambda e: None, deps=[tok], inc=False)


    def aop(fn, deps=(), inc=True):
        return p.op("scalar", fn, deps=deps, inc=inc)

    def bc3(ap2d, n_inner):
        return ap2d[:, :, None].broadcast_to([128, ap2d.shape[1], n_inner])


    def ln_stats2(x_parts, scr_bank, base, deps, eps=EPS, n_feat=1024, war_dep=None,
                  from_psum=False):
        ss0, ss1, sq0, sq1, s_me, s_va, s_rs, s_nm = range(base, base + 8)
        d = list(deps)
        if war_dep is not None:
            d.append(war_dep)
        if scr_bank.rtok is not None:
            d.append(scr_bank.rtok)
        p0, p1 = x_parts
        t1 = vop(lambda e: e.reduce_sum(sm(ss0), p0, axis=AX.X), deps=d)
        t1b = vop(lambda e: e.reduce_sum(sm(ss1), p1, axis=AX.X), deps=[t1])
        q1_ = aop(lambda e: e.activation(scr_bank.f32(0, 512), p0, AF.Square),
                  deps=[t1b])
        t2 = vop(lambda e: e.reduce_sum(sm(sq0), scr_bank.f32(0, 512), axis=AX.X),
                 deps=[q1_])
        q2_ = aop(lambda e: e.activation(scr_bank.f32(0, 512), p1, AF.Square),
                  deps=[t2])
        t2b = vop(lambda e: e.reduce_sum(sm(sq1), scr_bank.f32(0, 512), axis=AX.X),
                  deps=[q2_])
        scr_bank.wtok = t2b
        scr_bank.rtok = t2b
        t3 = vop(lambda e: e.tensor_add(sm(s_me), sm(ss0), sm(ss1)), deps=[t2b])
        t4 = vop(lambda e: e.scalar_tensor_tensor(
            sm(sq0), sm(sq0), 1.0, sm(sq1), op0=ALU.mult, op1=ALU.add), deps=[t3])
        t5 = vop(lambda e: e.tensor_mul(sm(s_va), sm(s_me), sm(s_me)), deps=[t4])
        t5b = vop(lambda e: e.scalar_tensor_tensor(
            sm(s_va), sm(sq0), float(n_feat), sm(s_va),
            op0=ALU.mult, op1=ALU.subtract), deps=[t5])
        t6 = aop(lambda e: e.activation(sm(s_rs), sm(s_va), AF.Ln, bias=N2EPS),
                 deps=[t5b])
        t6b = aop(lambda e: e.activation(sm(s_rs), sm(s_rs), AF.Exp, scale=-0.5,
                                         bias=LNN), deps=[t6])
        t7 = vop(lambda e: e.scalar_tensor_tensor(
            sm(s_nm), sm(s_me), -1.0 / n_feat, sm(s_rs), op0=ALU.mult,
            op1=ALU.mult), deps=[t6b])
        return t7, sm(s_rs), sm(s_nm)


    LN8 = math.log(QK_SCALE)
    tk = {"fmb_reader": None, "h2_reader": None, "wpart_free": None}
    cast_tok = {}

    def xs_fm_build(deps, block_cb=None):
        XS = XO
        z0 = vop(lambda e: e.memset(FMB.ap()[:, :, 0:1], 0.0), deps=deps)
        bounce = sv(16384, BF16, 2, 1024)
        b_rtok = [None, None]
        toks = []
        for t in range(NT):
            dd = list(deps)
            if b_rtok[t % 2] is not None:
                dd.append(b_rtok[t % 2])
            ct = vop(lambda e, t=t: e.tensor_copy(bounce[:, t % 2, :], XS.ap()[:, t, :]),
                     deps=dd)
            last_t = None
            for c in range(8):
                bk = banks[6 + (c % 2)]
                last_t = transpose(bk, bounce[:, t % 2, c * 128:(c + 1) * 128],
                                   deps=[ct, z0])
                toks.append(evac(bk, FMB.ap()[:, c, 1 + t * 128:1 + (t + 1) * 128]))
            b_rtok[t % 2] = last_t
            if block_cb is not None:
                block_cb(t, toks[-1])
            if paranoid:
                p.barrier()
        return toks[-1]

    def run_ffn_layer(s, l, xs_tok):
        XS = XO
        NW = max(T // 1024, 1)
        W_T = T // NW
        lgv = cv("lg", F32, 8)
        lbv = cv("lb", F32, 8)
        b2v = cv("b2", F32, 8)
        b1av = cv("b1a", F32, NJ)
        b1gv = cv("b1g", F32, NJ)
        ct1 = p.dma("sync", lgv, lg_d[l], deps=[xs_tok])
        ct2 = p.dma("sync", lbv, lb_d[l], deps=[xs_tok])
        ct3 = p.dma("sync", b2v, b2_d[l], deps=[xs_tok])
        ct4 = p.dma("sync", b1av, b1a_d[l], deps=[xs_tok])
        ct5 = p.dma("sync", b1gv, b1g_d[l], deps=[xs_tok])
        lconst = ct5

        dd = [xs_tok]
        if tk["fmb_reader"] is not None:
            dd.append(tk["fmb_reader"])
        fm_done = xs_fm_build(dd)

        QFM = tmv(0, BF16, 4, T)
        KFM = tmv(16384, BF16, 4, T)
        V_TM = rdv(0, BF16, NT, 512)
        wpart = sv(0, BF16, 8, 512)
        qs_f = sv(8192, F32, 512)
        qs_bf = sv(10240, BF16, 512)
        q_raw = sv(15360, BF16, 512)
        sq_b = sv(16384, BF16, 512)
        q1b = sv(17408, BF16, 512)
        t1b = sv(18432, BF16, 512)
        t2b_ = sv(19456, BF16, 512)
        qf_pp = sv(20480, BF16, 2, 512)
        smv = cv("smalls", F32, 64)

        wpart_rtok = [None]
        qkv_done = None
        tblbuf = [sv(11264 + i * 2048, BF16, 2, 512) for i in range(2)]
        tbl_rtok = [None, None]
        qf_rtok = [None, None]
        qs_rtok = [None]
        for tgt_i, tgt in enumerate(("q", "k", "v")):
            dd = [tk["wpart_free"]] if (tgt_i == 0 and tk.get("wpart_free")) \
                else [fm_done]
            if wpart_rtok[0] is not None:
                dd.append(wpart_rtok[0])
            wtok = p.dma("sync", wpart,
                         wqkv_scr[l][:, :, tgt_i * 512:(tgt_i + 1) * 512],
                         deps=dd + [cast_tok[f"wqkv{l}"]])
            if tgt == "q":
                qd = [lconst] + ([qs_rtok[0]] if qs_rtok[0] else [])
                qtok = p.dma("sync", qs_f, qs_d[l], deps=qd)
                cast_t = vop(lambda e: e.tensor_copy(qs_bf, qs_f), deps=[qtok])
                tbls = ("rot_cq", "rot_sq")
                expb = LN8
            elif tgt == "k":
                qd = [lconst] + ([qs_rtok[0]] if qs_rtok[0] else [])
                qtok = p.dma("sync", qs_f, ks_d[l], deps=qd)
                cast_t = vop(lambda e: e.tensor_copy(qs_bf, qs_f), deps=[qtok])
                tbls = ("rot_ck", "rot_sk")
                expb = 0.0
            last_tok = None
            for t in range(NT):
                bk = banks[t % 2]
                for cd in range(8):
                    c_off = 1 + t * 128 if cd < 4 else t * 128
                    lhsT = FMB.ap()[:, cd, c_off:c_off + 128]
                    mm(bk, 0, 512, lhsT, wpart[:, cd, :], cd == 0, cd == 7,
                       deps=[wtok, fm_done])
                wpart_rtok[0] = bk.wtok
                if tgt == "v":
                    evt = vop(lambda e, t=t, bk=bk: e.tensor_copy(
                        V_TM[:, t, :], bk.f32(0, 512)), deps=[bk.wtok])
                    bk.rtok = evt
                    last_tok = evt
                    continue
                td = []
                if tbl_rtok[t % 2] is not None:
                    td.append(tbl_rtok[t % 2])
                tca = p.dma("sync", tblbuf[t % 2][:, 0, :],
                            rot_d[tbls[0]][t * 128:(t + 1) * 128, :], deps=td)
                tcb = p.dma("sync", tblbuf[t % 2][:, 1, :],
                            rot_d[tbls[1]][t * 128:(t + 1) * 128, :], deps=td)
                ev = vop(lambda e, bk=bk: e.tensor_copy(q_raw, bk.f32(0, 512)),
                         deps=[bk.wtok])
                bk.rtok = ev
                sbase = 16 + 16 * (t % 2)
                rbase = 24 + 16 * (t % 2)
                s1 = vop(lambda e: e.tensor_mul(sq_b, q_raw, q_raw), deps=[ev])
                s2 = vop(lambda e, sb_=sbase: e.tensor_reduce(
                    smv[:, sb_:sb_ + 8], ar.view(SCR.off + 16384, BF16, 8, 64),
                    axis=AX.X, op=ALU.add), deps=[s1])
                s4 = aop(lambda e, sb_=sbase, rb_=rbase: e.activation(
                    smv[:, rb_:rb_ + 8], smv[:, sb_:sb_ + 8], AF.Ln, bias=TINYC),
                    deps=[s2])
                if tgt == "q":
                    s5 = aop(lambda e, rb_=rbase: e.activation(
                        smv[:, rb_:rb_ + 8], smv[:, rb_:rb_ + 8], AF.Exp, scale=-0.5,
                        bias=LN8C), deps=[s4])
                else:
                    s5 = aop(lambda e, rb_=rbase: e.activation(
                        smv[:, rb_:rb_ + 8], smv[:, rb_:rb_ + 8], AF.Exp, scale=-0.5),
                        deps=[s4])
                r1 = vop(lambda e: e.tensor_mul(q1b, q_raw, qs_bf), deps=[ev, cast_t])
                qs_rtok[0] = r1
                r2d = [r1, tca]
                if tk.get("pad_war"):
                    r2d.append(tk["pad_war"])
                r2 = vop(lambda e, t=t: e.tensor_mul(t1b, q1b, tblbuf[t % 2][:, 0, :]),
                         deps=r2d)
                shuf = ar.view(SCR.off + 17408, BF16, 8, 2, 32)[:, :, ::-1, :]
                r3 = vop(lambda e, t=t: e.tensor_tensor(
                    ar.view(SCR.off + 19456, BF16, 8, 2, 32),
                    shuf, tblbuf[t % 2][:, 1, :].rearrange("pp (h two d) -> pp h two d",
                                                           two=2, d=32),
                    op=ALU.mult), deps=[r1, tcb])
                r4 = vop(lambda e: e.tensor_tensor(t1b, t1b, t2b_, op=ALU.add),
                         deps=[r2, r3])
                qfd = [r4, s5]
                if qf_rtok[t % 2] is not None:
                    qfd.append(qf_rtok[t % 2])
                r5 = vop(lambda e, t=t, rb_=rbase: e.tensor_tensor(
                    ar.view(SCR.off + 20480 + (t % 2) * 1024, BF16, 8, 64),
                    ar.view(SCR.off + 18432, BF16, 8, 64),
                    smv[:, rb_:rb_ + 8][:, :, None].broadcast_to([128, 8, 64]),
                    op=ALU.mult), deps=qfd)
                tbl_rtok[t % 2] = r3
                if t == 0 and tgt == "q" and s == 0 and l == 0 and "dbg_qft0" in dbg_out:
                    dbg_dump("dbg_qraw0", q_raw, [ev])
                    dbg_dump("dbg_rstd0", smv[:, rbase:rbase + 8], [s5])
                    dbg_dump("dbg_q10", q1b, [r1])
                    dbg_dump("dbg_t10", t1b, [r4])
                    dbg_dump("dbg_t20", t2b_, [r3])
                    dbg_dump("dbg_qft0", ar.view(SCR.off + 20480, BF16, 512), [r5])
                DST = QFM if tgt == "q" else KFM
                ltt = None
                for c in range(4):
                    bk2 = banks[2 + ((t + c) % 2)]
                    ltt = transpose(bk2, qf_pp[:, t % 2, c * 128:(c + 1) * 128],
                                    deps=[r5])
                    last_tok = evac(bk2, DST[:, c, t * 128:(t + 1) * 128])
                qf_rtok[t % 2] = ltt
                if paranoid:
                    p.barrier()
                if tgt == "q" and s == 0 and l == 0 and "dbg_qfm_t0a" in dbg_out:
                    if t == 0:
                        p.dma("gpsimd", dbg_out["dbg_qfm_t0a"],
                              tmv(0, BF16, 4, T)[:, :, 0:128], deps=[last_tok])
                    if t == 2:
                        p.dma("gpsimd", dbg_out["dbg_qfm_t0b"],
                              tmv(0, BF16, 4, T)[:, :, 0:128], deps=[last_tok])
            if tgt == "q" and s == 0 and l == 0 and "dbg_qfm_early" in dbg_out:
                dbg_dump("dbg_qfm_early", ar.view(TM16.off, BF16, 4 * T), [last_tok])
        qkv_done = last_tok
        tk["fmb_reader"] = qkv_done
        if s == 0 and l == 0:
            for nm_, src_ in (("dbg_qfm", ar.view(TM16.off, BF16, 4 * T)),
                              ("dbg_kfm", ar.view(TM16.off + 16384, BF16, 4 * T)),
                              ("dbg_vtm", ar.view(RD.off, BF16, NT * 512))):
                if nm_ in dbg_out:
                    dbg_dump(nm_, src_, [qkv_done])

        O_FM = rdv(16384, BF16, 4, T)
        P_buf = [rdv(32768, BF16, 640), rdv(34048, BF16, 640)]
        PT_buf = [rdv(35328, BF16, 640), rdv(36608, BF16, 640)]
        qfpad = [rdv(37888, BF16, 128), rdv(38144, BF16, 128)]
        zq = vop(lambda e: e.memset(rdv(37888, BF16, 256), 0.0), deps=[qkv_done])
        qf_guard = [zq, zq]
        amlo = cv("amlo", F32, 128)
        amhi = cv("amhi", F32, 128)
        o_ev_tok = None
        p_guard = [None, None]
        pt_guard = [None, None]
        for hp in range(4):
            for it in range(NT):
                j0c = max(it - 4, 0)
                nch = it - j0c + 1
                jn = 128 * nch
                n1 = min(jn, 512)
                for X in range(2):
                    par = X * 64
                    head = 2 * hp + X
                    sb0 = banks[0 + 2 * X]
                    sb1 = banks[1 + 2 * X]
                    cpq = vop(lambda e, X=X, par=par, hp=hp, it=it: e.tensor_copy(
                        qfpad[X][par:par + 64, :],
                        QFM[par:par + 64, hp, it * 128:(it + 1) * 128]),
                        deps=[qkv_done, qf_guard[X]])
                    lhsT = qfpad[X]
                    mm(sb0, 0, n1, lhsT,
                       KFM[:, hp, j0c * 128:j0c * 128 + n1],
                       True, True, deps=[qkv_done, cpq])
                    if jn > 512:
                        qf_guard[X] = mm(sb1, 0, jn - 512, lhsT,
                                         KFM[:, hp, j0c * 128 + 512:j0c * 128 + jn],
                                         True, True, deps=[qkv_done, cpq])
                    else:
                        qf_guard[X] = sb0.wtok
                    mtoks = []
                    if it >= 4:
                        mk = vop(lambda e, sb0=sb0: e.tensor_tensor(
                            sb0.f32(0, 128), sb0.f32(0, 128), amlo, op=ALU.add),
                            deps=[sb0.wtok])
                        sb0.rtok = mk
                        sb0.wtok = mk
                        mtoks.append(mk)
                    if jn <= 512:
                        mk = vop(lambda e, sb0=sb0, jn=jn: e.tensor_tensor(
                            sb0.f32(jn - 128, jn), sb0.f32(jn - 128, jn), amhi,
                            op=ALU.add), deps=[sb0.wtok])
                        sb0.rtok = mk
                        sb0.wtok = mk
                        mtoks.append(mk)
                    else:
                        mk = vop(lambda e, sb1=sb1, jn=jn: e.tensor_tensor(
                            sb1.f32(jn - 640, jn - 512), sb1.f32(jn - 640, jn - 512),
                            amhi, op=ALU.add), deps=[sb1.wtok])
                        sb1.rtok = mk
                        sb1.wtok = mk
                        mtoks.append(mk)
                    sl = 16 + 8 * X + 16 * (it % 2)
                    x1 = vop(lambda e, sb0=sb0, n1=n1, sl=sl: e.reduce_max(
                        smv[:, sl:sl + 1], sb0.f32(0, n1), axis=AX.X),
                        deps=[sb0.wtok] + mtoks)
                    sb0.rtok = x1
                    mx = x1
                    if jn > 512:
                        x2 = vop(lambda e, sb1=sb1, jn=jn, sl=sl: e.reduce_max(
                            smv[:, sl + 1:sl + 2], sb1.f32(0, jn - 512), axis=AX.X),
                            deps=[sb1.wtok] + mtoks)
                        sb1.rtok = x2
                        mx = vop(lambda e, sl=sl: e.tensor_max(
                            smv[:, sl:sl + 1], smv[:, sl:sl + 1], smv[:, sl + 1:sl + 2]),
                            deps=[x1, x2])
                    ng = vop(lambda e, sl=sl: e.tensor_scalar_mul(
                        smv[:, sl + 2:sl + 3], smv[:, sl:sl + 1], -1.0), deps=[mx])
                    e1d = [ng]
                    if p_guard[X] is not None:
                        e1d.append(p_guard[X])
                    e1 = aop(lambda e, X=X, sb0=sb0, n1=n1, sl=sl: e.activation(
                        P_buf[X][:, 0:n1], sb0.f32(0, n1), AF.Exp,
                        bias=smv[:, sl + 2:sl + 3]), deps=e1d)
                    sb0.rtok = e1
                    lastexp = e1
                    if jn > 512:
                        e2 = aop(lambda e, X=X, sb1=sb1, jn=jn, sl=sl: e.activation(
                            P_buf[X][:, 512:jn], sb1.f32(0, jn - 512), AF.Exp,
                            bias=smv[:, sl + 2:sl + 3]), deps=[e1])
                        sb1.rtok = e2
                        lastexp = e2
                    sumtok = vop(lambda e, X=X, jn=jn, sl=sl: e.reduce_sum(
                        smv[:, sl + 3:sl + 4], P_buf[X][:, 0:jn], axis=AX.X),
                        deps=[lastexp])
                    rc = vop(lambda e, sl=sl: e.reciprocal(
                        smv[:, sl + 5:sl + 6], smv[:, sl + 3:sl + 4]), deps=[sumtok])
                    ps = vop(lambda e, X=X, jn=jn, sl=sl: e.tensor_scalar_mul(
                        P_buf[X][:, 0:jn], P_buf[X][:, 0:jn], smv[:, sl + 5:sl + 6]),
                        deps=[rc])
                    pt_toks = []
                    ltr = None
                    for c in range(nch):
                        bk = banks[4]
                        evd = [ps]
                        if pt_guard[X] is not None:
                            evd.append(pt_guard[X])
                        ltr = transpose(bk, P_buf[X][:, c * 128:(c + 1) * 128], deps=evd)
                        pt_toks.append(evac(bk, PT_buf[X][:, c * 128:(c + 1) * 128]))
                    p_guard[X] = ltr
                    for c in range(nch):
                        startf = (it % 4 == 0) and (c == 0)
                        stopf = (it % 4 == 3) and (c == nch - 1)
                        out_ap = banks[5].t.ap()[par:par + 64,
                                                 (it % 4) * 128:(it % 4) * 128 + 128]
                        avt = mm_ap(banks[5], out_ap,
                              V_TM[:, j0c + c, head * 64:(head + 1) * 64],
                              PT_buf[X][:, c * 128:(c + 1) * 128],
                              startf, stopf, deps=[pt_toks[c], qkv_done],
                              tile_position=(0, par), inc=True,
                              skip_group_check=(X == 1))
                        pt_guard[X] = avt
                if paranoid:
                    p.barrier()
                if it % 4 == 3:
                    o_ev_tok = vop(lambda e, hp=hp, it=it: e.tensor_copy(
                        O_FM[:, hp, (it - 3) * 128:(it + 1) * 128],
                        banks[5].f32(0, 512)), deps=[banks[5].wtok])
                    banks[5].rtok = o_ev_tok
        attn_o_done = o_ev_tok
        if "dbg_ofm" in dbg_out and s == 0 and l == 0:
            dbg_dump("dbg_ofm", ar.view(RD.off + 16384, BF16, 4 * T), [attn_o_done])

        fwo_sb = sv(0, BF16, 4, 1024)
        fw = p.dma("sync", fwo_sb, fwo_scr[l],
                   deps=([wpart_rtok[0]] if wpart_rtok[0] else []) +
                        [cast_tok[f"fwo{l}"]])
        res_tok = None
        for t in range(NT):
            for c in range(4):
                lhsT = O_FM[:, c, t * 128:(t + 1) * 128]
                mm(banks[6], 0, 512, lhsT, fwo_sb[:, c, 0:512], c == 0, c == 3,
                   deps=[fw, attn_o_done])
                mm(banks[7], 0, 512, lhsT, fwo_sb[:, c, 512:1024], c == 0, c == 3,
                   deps=[fw, attn_o_done])
            a1 = vop(lambda e, t=t: e.tensor_tensor(
                XS.ap()[:, t, 0:512], XS.ap()[:, t, 0:512], banks[6].f32(0, 512),
                op=ALU.add), deps=[banks[6].wtok, xs_tok])
            banks[6].rtok = a1
            a2 = vop(lambda e, t=t: e.tensor_tensor(
                XS.ap()[:, t, 512:1024], XS.ap()[:, t, 512:1024], banks[7].f32(0, 512),
                op=ALU.add), deps=[banks[7].wtok, a1])
            banks[7].rtok = a2
            res_tok = a2
            if paranoid:
                p.barrier()

        rstd_bc = rdv(0, F32, 2048)
        mrstd_bc = rdv(8192, F32, 2048)
        sq_scr = [sv(17408, BF16, 512), sv(18432, BF16, 512)]
        svv = sv(15360, F32, 512)
        h2v = tmv(0, BF16, 8, T)
        t1h = sv(19456, BF16, 512)
        h2_toks = []
        st = {"bcd": None, "prev_sv": None, "prevh": None}

        def stats_h2_block(b, ftok):
            bA = banks[(b % 2) * 2]
            bB = banks[(b % 2) * 2 + 1]
            sq_rd = None
            for c in range(8):
                off = (1 if c < 4 else 0) + b * 512
                mm(bA, 0, 512, ONES_MAT, FMB.ap()[:, c, off:off + 512],
                   c == 0, c == 7, deps=[ftok])
                dd = [ftok]
                if sq_rd is not None:
                    dd.append(sq_rd)
                sqt = vop(lambda e, c=c, off=off, b=b: e.tensor_mul(
                    sq_scr[c % 2], FMB.ap()[:, c, off:off + 512],
                    FMB.ap()[:, c, off:off + 512]), deps=dd)
                sq_rd = mm(bB, 0, 512, ONES_MAT, sq_scr[c % 2],
                           c == 0, c == 7, deps=[sqt], inc=True)
            mrs = mrstd_bc[:, b * 512:(b + 1) * 512]
            rss = rstd_bc[:, b * 512:(b + 1) * 512]
            dd = [bA.wtok, bB.wtok]
            if st["prev_sv"] is not None:
                dd.append(st["prev_sv"])
            n1_ = vop(lambda e, mrs=mrs, bA=bA: e.tensor_scalar_mul(
                mrs, bA.f32(0, 512), 1.0 / 1024.0), deps=dd)
            bA.rtok = n1_
            n2_ = vop(lambda e, rss=rss, bB=bB: e.tensor_scalar_mul(
                rss, bB.f32(0, 512), 1.0 / 1024.0), deps=dd + [n1_])
            bB.rtok = n2_
            n3_ = vop(lambda e, mrs=mrs: e.tensor_mul(svv, mrs, mrs), deps=[n2_])
            n4_ = vop(lambda e, rss=rss: e.tensor_sub(rss, rss, svv), deps=[n3_])
            n5_ = aop(lambda e, rss=rss: e.activation(rss, rss, AF.Ln, bias=EPSC),
                      deps=[n4_])
            n6_ = aop(lambda e, rss=rss: e.activation(rss, rss, AF.Exp, scale=-0.5),
                      deps=[n5_])
            n7_ = vop(lambda e, mrs=mrs, rss=rss: e.tensor_mul(mrs, mrs, rss),
                      deps=[n6_])
            st["prev_sv"] = n3_
            st["bcd"] = n7_
            for c in range(8):
                off = (1 if c < 4 else 0) + b * 512
                dd = [ftok, n7_]
                if st["prevh"]:
                    dd.append(st["prevh"])
                g1_ = vop(lambda e, c=c, b=b, off=off: e.tensor_tensor(
                    t1h, FMB.ap()[:, c, off:off + 512],
                    rstd_bc[:, b * 512:(b + 1) * 512], op=ALU.mult), deps=dd)
                g2_ = vop(lambda e, c=c, b=b: e.tensor_sub(
                    t1h, t1h, mrstd_bc[:, b * 512:(b + 1) * 512]), deps=[g1_])
                g3_ = vop(lambda e, c=c, b=b: e.scalar_tensor_tensor(
                    h2v[:, c, b * 512:(b + 1) * 512], t1h, lgv[:, c:c + 1],
                    lbv[:, c:c + 1].broadcast_to([128, 512]),
                    op0=ALU.mult, op1=ALU.add), deps=[g2_, lconst])
                st["prevh"] = g3_
                h2_toks.append(g3_)

        def fm_cb(t, tile_tok):
            if t % 4 == 3:
                stats_h2_block(t // 4, tile_tok)

        dd = [res_tok]
        if tk["fmb_reader"] is not None:
            dd.append(tk["fmb_reader"])
        fm2_done = xs_fm_build(dd, block_cb=fm_cb)
        tk["fmb_reader"] = fm2_done
        bcd = st["bcd"]
        h2_done = h2_toks[-1]
        tk["h2_reader"] = None
        if "dbg_h2" in dbg_out and s == 0 and l == 0:
            dbg_dump("dbg_h2", ar.view(TM16.off, BF16, 8 * T), [h2_done])

        final_tok = None
        w1a_buf = [sv(0, BF16, 8, 128), sv(4096, BF16, 8, 128)]
        w1g_buf = [sv(2048, BF16, 8, 128), sv(6144, BF16, 8, 128)]
        w2_buf = [sv(8192, BF16, 22, 128), sv(13824, BF16, 22, 128)]
        a_sb = sv(19456, BF16, 1024)
        g_sb = sv(21504, BF16, 1024)
        zb_sb = sv(19456, BF16, 1024)
        w1_rtok = [None, None]
        w2_rtok = [None, None]
        zb_guard = [None]
        for w in range(NW):
            ag_done = []
            for j in range(NJ):
                nj = 128 if j < NJ - 1 else LAST_J
                dd = [res_tok]
                if w1_rtok[j % 2] is not None:
                    dd.append(w1_rtok[j % 2])
                da = p.dma("sync", w1a_buf[j % 2][:, :, 0:nj],
                           w1_scr[l][:, :, j * 128:j * 128 + nj],
                           deps=dd + [cast_tok[f"w1{l}"]])
                dg = p.dma("sync", w1g_buf[j % 2][:, :, 0:nj],
                           w1_scr[l][:, :, HID + j * 128:HID + j * 128 + nj],
                           deps=dd + [cast_tok[f"w1{l}"]])
                ba = [banks[4 * (j % 2) + 0], banks[4 * (j % 2) + 1]]
                bg = [banks[4 * (j % 2) + 2], banks[4 * (j % 2) + 3]]
                for cd in range(8):
                    rhs0 = h2v[:, cd, w * W_T:w * W_T + 512]
                    rhs1 = h2v[:, cd, w * W_T + 512:w * W_T + 1024] if W_T > 512 else None
                    la = w1a_buf[j % 2][:, cd, 0:nj]
                    lg_ = w1g_buf[j % 2][:, cd, 0:nj]
                    mm_ap(ba[0], ba[0].t.ap()[0:nj, 0:512], la, rhs0,
                          cd == 0, cd == 7, deps=[da, h2_done], inc=(cd == 7))
                    mm_ap(bg[0], bg[0].t.ap()[0:nj, 0:512], lg_, rhs0,
                          cd == 0, cd == 7, deps=[dg, h2_done], inc=(cd == 7))
                    if rhs1 is not None:
                        mm_ap(ba[1], ba[1].t.ap()[0:nj, 0:512], la, rhs1,
                              cd == 0, cd == 7, deps=[da, h2_done], inc=(cd == 7))
                        mm_ap(bg[1], bg[1].t.ap()[0:nj, 0:512], lg_, rhs1,
                              cd == 0, cd == 7, deps=[dg, h2_done], inc=(cd == 7))
                nwb = 2 if W_T > 512 else 1
                etoks = []
                for q_ in range(nwb):
                    ead = [ba[q_].wtok, lconst]
                    if zb_guard[0] is not None:
                        ead.append(zb_guard[0])
                    ea = aop(lambda e, j=j, q_=q_, nj=nj, ba=ba: e.activation(
                        a_sb[0:nj, q_ * 512:(q_ + 1) * 512], ba[q_].t.ap()[0:nj, 0:512],
                        AF.Identity, bias=b1av[0:nj, j:j + 1]), deps=ead)
                    ba[q_].rtok = ea
                    eg = aop(lambda e, j=j, q_=q_, nj=nj, bg=bg: e.activation(
                        g_sb[0:nj, q_ * 512:(q_ + 1) * 512], bg[q_].t.ap()[0:nj, 0:512],
                        AF.Gelu, bias=b1gv[0:nj, j:j + 1]), deps=[bg[q_].wtok, lconst])
                    bg[q_].rtok = eg
                    etoks += [ea, eg]
                agt = vop(lambda e, j=j, nj=nj: e.tensor_mul(
                    rdv(j * 2048, BF16, 1024)[0:nj, 0:W_T],
                    a_sb[0:nj, 0:W_T], g_sb[0:nj, 0:W_T]), deps=etoks)
                ag_done.append(agt)
                w1_rtok[j % 2] = agt
                if paranoid:
                    p.barrier()
            tk["wpart_free"] = ag_done[-1]
            for fc in range(8):
                dd = [ag_done[0]]
                if w2_rtok[fc % 2] is not None:
                    dd.append(w2_rtok[fc % 2])
                dw0 = p.dma("sync", w2_buf[fc % 2][:],
                            w2_scr[l][:, :, fc * 128:(fc + 1) * 128],
                            deps=dd + [cast_tok[f"w2{l}"]])
                dw1 = dw0
                zb = [banks[2 * (fc % 2)], banks[2 * (fc % 2) + 1]]
                for j in range(NJ):
                    nj = 128 if j < NJ - 1 else LAST_J
                    lhsT = w2_buf[fc % 2][0:nj, j, :]
                    rhs = rdv(j * 2048, BF16, 1024)[0:nj, 0:W_T]
                    mm(zb[0], 0, 512, lhsT, rhs[:, 0:512], j == 0, j == NJ - 1,
                       deps=[dw0, dw1, ag_done[j]])
                    if W_T > 512:
                        mm(zb[1], 0, 512, lhsT, rhs[:, 512:1024], j == 0, j == NJ - 1,
                           deps=[dw0, dw1, ag_done[j]])
                w2_rtok[fc % 2] = zb[0].wtok
                nwb = 2 if W_T > 512 else 1
                ztoks = []
                for q_ in range(nwb):
                    zt = aop(lambda e, q_=q_, fc=fc, zb=zb: e.activation(
                        zb_sb[:, q_ * 512:(q_ + 1) * 512], zb[q_].f32(0, 512),
                        AF.Identity, bias=b2v[:, fc:fc + 1]),
                        deps=[zb[q_].wtok, lconst, ag_done[-1]])
                    zb[q_].rtok = zt
                    ztoks.append(zt)
                for ttl in range(W_T // 128):
                    bk = banks[4 + ttl % 2]
                    zb_guard[0] = transpose(bk, zb_sb[:, ttl * 128:(ttl + 1) * 128],
                                            deps=ztoks)
                    tglob = w * (W_T // 128) + ttl
                    rtok = vop(lambda e, tglob=tglob, fc=fc, bk=bk: e.tensor_tensor(
                        XS.ap()[:, tglob, fc * 128:(fc + 1) * 128],
                        XS.ap()[:, tglob, fc * 128:(fc + 1) * 128],
                        bk.t.ap()[:, 0:64].bitcast(BF16), op=ALU.add),
                        deps=[bk.wtok, res_tok])
                    bk.rtok = rtok
                    final_tok = rtok
                if paranoid:
                    p.barrier()
        if "dbg_xsfin" in dbg_out and s == 0 and l == 1:
            tok_ = p.dma("sync", dbg_out["dbg_xsfin"].rearrange("(nt pp) d -> pp nt d", pp=128),
                         XS.ap(), deps=[final_tok])
            for eng_ in ENGS:
                p.op(eng_, lambda e: None, deps=[tok_], inc=False)
        return final_tok

    def early_out(s, dep):
        toks = []
        for t in range(NT):
            toks.append(p.dma("sync",
                              y_out[s].rearrange("(nt p) d -> p nt d", p=128)[:, t, :],
                              XO.ap()[:, t, :], deps=[dep]))
        return toks

    def run_seq(s, prev):
        fmz = vop(lambda e: e.memset(FMB.ap(), 0.0), deps=prev)
        xview = x_in[s].rearrange("(nt p) d -> p nt d", p=128)
        xo_tile = []
        for t in range(NT):
            xo_tile.append(p.dma("sync", XO.ap()[:, t, :], xview[:, t, :],
                                 deps=[ctok] + list(prev)))
        xo_tok = xo_tile[-1]

        if stop_phase <= 0:
            return early_out(s, xo_tok)
        stage = tmv(0, BF16, 8, 1024)
        wt_dst = {"q": rdv(0, BF16, 8, 1024), "k": rdv(16384, BF16, 8, 1024),
                  "v": sv(0, BF16, 8, 1024)}
        wt_tok = {}
        stage_rtoks = [None]
        for nm, wd in (("k", wk_d), ("q", wq_d), ("v", wv_d)):
            dep = [ctok] + list(prev) + \
                ([t for t in stage_rtoks if t] if stage_rtoks else [])
            if s > 0:
                wt_tok[nm] = p.dma("sync", wt_dst[nm], wtT_scr[nm],
                                   deps=[t for t in dep if t] +
                                        [tk["wtT_" + nm]])
                continue
            stok = p.dma("gpsimd", stage,
                         wd.rearrange("(cf p) d -> p cf d", p=128),
                         deps=[t for t in dep if t])
            stage_rtoks = []
            etoks = []
            for cf in range(8):
                for cd in range(8):
                    bk = banks[6 + (cd % 2)]
                    tt = transpose(bk, stage[:, cf, cd * 128:(cd + 1) * 128],
                                   deps=[stok])
                    et = evac(bk, wt_dst[nm][:, cd, cf * 128:(cf + 1) * 128])
                    etoks.append(et)
                    stage_rtoks = [tt]
            wt_tok[nm] = etoks[-1]
            tk["wtT_" + nm] = p.dma("gpsimd", wtT_scr[nm], wt_dst[nm],
                                    deps=[etoks[-1]])
            if paranoid:
                p.barrier()
            if nm == "q" and "dbg_wqT_early" in dbg_out and s == 0:
                dbg_dump("dbg_wqT_early", rdv(0, BF16, 8 * 1024), [etoks[-1]])
            if nm == "q" and "dbg_idn" in dbg_out and s == 0:
                dbg_dump("dbg_idn", IDN, [etoks[-1]])

        if s == 0:
            cast_tok["wqkv0"] = p.dma("gpsimd", wqkv_scr[0],
                wqkv_d[0].rearrange("(cd pp) f -> pp cd f", pp=128))
            cast_tok["fwo0"] = p.dma("gpsimd", fwo_scr[0],
                fwo_d[0].rearrange("(c pp) f -> pp c f", pp=128))
            cast_tok["w10"] = p.dma("gpsimd", w1_scr[0],
                w1_d[0].rearrange("(cd pp) f -> pp cd f", pp=128))
            p.dma("gpsimd", w2_scr[0, :, 0:21, :],
                  w2_d[0, 0:21 * 128, :].rearrange("(cj pp) f -> pp cj f", pp=128))
            cast_tok["w20"] = p.dma("gpsimd", w2_scr[0, 0:LAST_J, 21, :],
                                    w2_d[0, 21 * 128:HID, :])
            cast_tok["wqkv1"] = p.dma("gpsimd", wqkv_scr[1],
                wqkv_d[1].rearrange("(cd pp) f -> pp cd f", pp=128))
            cast_tok["fwo1"] = p.dma("gpsimd", fwo_scr[1],
                fwo_d[1].rearrange("(c pp) f -> pp c f", pp=128))
            cast_tok["w11"] = p.dma("gpsimd", w1_scr[1],
                w1_d[1].rearrange("(cd pp) f -> pp cd f", pp=128))
            p.dma("gpsimd", w2_scr[1, :, 0:21, :],
                  w2_d[1, 0:21 * 128, :].rearrange("(cj pp) f -> pp cj f", pp=128))
            cast_tok["w21"] = p.dma("gpsimd", w2_scr[1, 0:LAST_J, 21, :],
                                    w2_d[1, 21 * 128:HID, :])
            cast_tok["wo"] = p.dma("gpsimd", wattn_scr["o"],
                wo_d.rearrange("(cf p) d -> p cf d", p=128))
            zb = ar.view(SCR.off + 18432, BF16, 1024)
            zt = vop(lambda e: e.memset(zb, 0.0))
            p.dma("gpsimd", w2_scr[0, LAST_J:128, 21, :], zb[LAST_J:128, :],
                  deps=[zt])
            tk["pad_war"] = p.dma("gpsimd", w2_scr[1, LAST_J:128, 21, :],
                                  zb[LAST_J:128, :], deps=[zt])

        if stop_phase <= 1:
            return early_out(s, wt_tok["v"])
        bounce = sv(16384, BF16, 2, 1024)
        fm_toks = []
        bounce_rtok = [None, None]
        ln1_war = [None, None]
        for t in range(NT):
            xt = XO.ap()[:, t, :]
            stok, rstd, negmr = ln_stats2([xt[:, 0:512], xt[:, 512:1024]], banks[4],
                                          0 if t % 2 == 0 else 8,
                                          deps=[xo_tile[t]],
                                          war_dep=ln1_war[t % 2])
            bb = bounce[:, t % 2, :]
            dd = [stok]
            if bounce_rtok[t % 2] is not None:
                dd.append(bounce_rtok[t % 2])
            btok = vop(lambda e, bb=bb, r=rstd, n=negmr, t=t:
                       e.tensor_scalar(bb, XO.ap()[:, t, :], r, n,
                                       op0=ALU.mult, op1=ALU.add),
                       deps=dd)
            ln1_war[t % 2] = btok
            if t == 0 and "dbg_smalls" in dbg_out and s == 0:
                dbg_dump("dbg_smalls", cv("smalls", F32, 64), [stok])
                dbg_dump("dbg_bounce", bb, [btok])
            last_t = None
            for c in range(8):
                bk = banks[6 + (c % 2)]
                tt = transpose(bk, bb[:, c * 128:(c + 1) * 128], deps=[btok])
                et = evac(bk, FMB.ap()[:, c, 1 + t * 128: 1 + (t + 1) * 128])
                fm_toks.append(et)
                last_t = tt
            bounce_rtok[t % 2] = last_t
            if paranoid:
                p.barrier()
        fm_done = fm_toks[-1]
        if "dbg_lnxfm" in dbg_out and s == 0:
            dbg_dump("dbg_lnxfm", ar.view(FMB.off, BF16, 8 * FMW), [fm_done])

        if stop_phase <= 2:
            return early_out(s, fm_done)
        q_tm = rdv(32768, BF16, 16, 64)
        k_tm = rdv(32768 + 2048, BF16, 16, 64)
        v_tm = rdv(32768 + 4096, BF16, 16, 64)
        qfm_c = rdv(32768 + 6144, BF16, 8, 128)
        kpad = rdv(32768 + 8192, BF16, 16, 128)
        a_sb = rdv(32768 + 12288, BF16, 16, 128)
        s_cur = sv(20480, F32, 8, 64)
        s_tmp = sv(22528, F32, 8, 64)
        s_b16 = sv(16384, BF16, 16, 64)
        gq = cv("gq", F32, 16)
        gk = cv("gk", F32, 16)
        gC1 = cv("gC1", F32, 8)
        gC = cv("gC", F32, 8)
        g1 = cv("g1", F32, 8)
        maskk = cv("maskk", F32, 128)

        st_init = vop(lambda e: e.memset(s_cur, 0.0))
        kp_init = vop(lambda e: e.memset(kpad, 0.0))
        sb_zero = vop(lambda e: e.memset(s_b16, 0.0))
        banks[4].rtok = banks[4].wtok
        sb_init = None

        prev_state_tok = st_init
        prev_sb_tok = None
        y_toks = []
        for t in range(NT):
            deps_w = [wt_tok["q"], wt_tok["k"], wt_tok["v"], fm_done]
            for cd in range(8):
                lhsT = FMB.ap()[:, cd, 1 + t * 128: 1 + (t + 1) * 128]
                first = (cd == 0)
                last = (cd == 7)
                mm(banks[0], 0, 512, lhsT, wt_dst["q"][:, cd, 0:512], first, last, deps=deps_w)
                mm(banks[1], 0, 512, lhsT, wt_dst["q"][:, cd, 512:1024], first, last, deps=deps_w)
                mm(banks[2], 0, 512, lhsT, wt_dst["k"][:, cd, 0:512], first, last, deps=deps_w)
                mm(banks[3], 0, 512, lhsT, wt_dst["k"][:, cd, 512:1024], first, last, deps=deps_w)
                mm(banks[4], 0, 512, lhsT, wt_dst["v"][:, cd, 0:512], first, last, deps=deps_w)
                mm(banks[5], 0, 512, lhsT, wt_dst["v"][:, cd, 512:1024], first, last, deps=deps_w)
            eq0 = vop(lambda e: e.tensor_tensor(
                q_tm[:, 0:8, :], banks[0].f32v(0, 8, 64), bc3(gq[:, 0:8], 64), op=ALU.mult),
                deps=[banks[0].wtok])
            banks[0].rtok = eq0
            eq1 = vop(lambda e: e.tensor_tensor(
                q_tm[:, 8:16, :], banks[1].f32v(0, 8, 64), bc3(gq[:, 8:16], 64), op=ALU.mult),
                deps=[banks[1].wtok])
            banks[1].rtok = eq1
            ek0 = vop(lambda e: e.tensor_tensor(
                k_tm[:, 0:8, :], banks[2].f32v(0, 8, 64), bc3(gk[:, 0:8], 64), op=ALU.mult),
                deps=[banks[2].wtok])
            banks[2].rtok = ek0
            ek1 = vop(lambda e: e.tensor_tensor(
                k_tm[:, 8:16, :], banks[3].f32v(0, 8, 64), bc3(gk[:, 8:16], 64), op=ALU.mult),
                deps=[banks[3].wtok])
            banks[3].rtok = ek1
            ev0 = vop(lambda e: e.tensor_copy(v_tm[:, 0:8, :], banks[4].f32v(0, 8, 64)),
                      deps=[banks[4].wtok])
            banks[4].rtok = ev0
            ev1 = vop(lambda e: e.tensor_copy(v_tm[:, 8:16, :], banks[5].f32v(0, 8, 64)),
                      deps=[banks[5].wtok])
            banks[5].rtok = ev1
            qf_toks = []
            for c in range(8):
                bk = banks[6 + (c % 2)]
                tt = transpose(bk, ar.view(RD.off + 32768, BF16, 1024)[:, c * 128:(c + 1) * 128],
                               deps=[eq0, eq1])
                qf_toks.append(evac(bk, qfm_c[:, c, :]))
            kf_toks = []
            for c in range(8):
                bk = banks[6 + (c % 2)]
                tt = transpose(bk, ar.view(RD.off + 32768 + 2048, BF16, 1024)[:, c * 128:(c + 1) * 128],
                               deps=[ek0, ek1, kp_init])
                e1_ = vop(lambda e, bk=bk, c=c: e.tensor_copy(
                    kpad[0:64, 2 * c, :], bk.t.ap()[0:64, 0:64].bitcast(BF16)),
                    deps=[bk.wtok])
                e2_ = vop(lambda e, bk=bk, c=c: e.tensor_copy(
                    kpad[64:128, 2 * c + 1, :], bk.t.ap()[64:128, 0:64].bitcast(BF16)),
                    deps=[e1_])
                bk.rtok = e2_
                kf_toks.append(e2_)
            qk_done = qf_toks + kf_toks
            if retpart == "a":
                yy = vop(lambda e, t=t: e.memset(TM16.ap()[:, t, :], 0.0),
                         deps=[eq0, eq1, ek0, ek1, ev0, ev1] + qk_done)
                y_toks += [yy, yy]
                prev_state_tok = st_init
                prev_sb_tok = None
                continue
            sc_toks = {}
            for h in range(16):
                bk = banks[h // 4]
                lo = (h % 4) * 128
                sc_toks[h] = mm(bk, lo, lo + 128, kpad[:, h, :], qfm_c[:, h // 2, :],
                                True, True, deps=qk_done)
            a_toks = {}
            for b in range(4):
                bk = banks[b]
                tok = vop(lambda e, bk=bk, b=b: e.tensor_tensor(
                    a_sb[:, 4 * b:4 * b + 4, :], bk.f32v(0, 4, 128),
                    maskk[:, None, :].broadcast_to([128, 4, 128]), op=ALU.mult),
                    deps=[bk.wtok])
                bk.rtok = tok
                a_toks[b] = tok
            if retpart == "b":
                yy = vop(lambda e, t=t: e.memset(TM16.ap()[:, t, :], 0.0),
                         deps=[a_toks[0], a_toks[1], a_toks[2], a_toks[3]])
                y_toks += [yy, yy]
                continue
            for h in range(16):
                yb = banks[6 + h // 8]
                lo = (h % 8) * 64
                par = (h % 2) * 64
                d_in = [a_toks[h // 4], ev0, ev1]
                first_of_bank = (h % 8 == 0)
                last_of_bank = (h % 8 == 7)
                intra_is_last = last_of_bank and (t == 0)
                mm(yb, lo, lo + 64, a_sb[:, h, :], v_tm[:, h, :],
                   first_of_bank, intra_is_last, deps=d_in, inc=intra_is_last)
                if t > 0 and h % 2 == 1:
                    mm(yb, lo - 64, lo + 64, qfm_c[:, h // 2, :],
                       s_b16[:, h - 1:h + 1, :], False, h % 8 == 7,
                       deps=[prev_sb_tok], inc=(h % 8 == 7))
            ye0 = vop(lambda e, t=t: e.tensor_copy(TM16.ap()[:, t, 0:512],
                                                   banks[6].f32(0, 512)),
                      deps=[banks[6].wtok])
            banks[6].rtok = ye0
            ye1 = vop(lambda e, t=t: e.tensor_copy(TM16.ap()[:, t, 512:1024],
                                                   banks[7].f32(0, 512)),
                      deps=[banks[7].wtok])
            banks[7].rtok = ye1
            y_toks += [ye0, ye1]
            if retpart == "c":
                continue
            st_mm_toks = []
            for h in range(16):
                par = (h % 2) * 64
                lo = (h // 2) * 64
                out_ap = banks[4].t.ap()[par:par + 64, lo:lo + 64]
                tok = mm_ap(banks[4], out_ap, k_tm[:, h, :], v_tm[:, h, :],
                            (h <= 1), (h >= 14), deps=[ek0, ek1, ev0, ev1],
                            tile_position=(0, par), inc=(h == 15),
                            skip_group_check=(h % 2 == 1))
            st_wtok = banks[4].wtok
            e1 = vop(lambda e: e.tensor_tensor(
                s_tmp, banks[4].f32v(0, 8, 64), bc3(gC1, 64), op=ALU.mult),
                deps=[st_wtok, prev_state_tok])
            banks[4].rtok = e1
            e2 = vop(lambda e: e.tensor_tensor(
                s_cur, s_cur, bc3(gC, 64), op=ALU.mult), deps=[e1, prev_state_tok])
            e3 = vop(lambda e: e.tensor_tensor(s_cur, s_cur, s_tmp, op=ALU.add),
                     deps=[e2])
            e4a = vop(lambda e: e.tensor_tensor(
                s_b16[0:64, 0:16:2, :], s_cur[0:64, :, :],
                g1[0:64, :, None].broadcast_to([64, 8, 64]), op=ALU.mult),
                deps=[e3, sb_zero])
            e4 = vop(lambda e: e.tensor_tensor(
                s_b16[64:128, 1:16:2, :], s_cur[64:128, :, :],
                g1[64:128, :, None].broadcast_to([64, 8, 64]), op=ALU.mult),
                deps=[e4a])
            prev_state_tok = e3
            prev_sb_tok = e4
            if paranoid:
                p.barrier()
        y_done = y_toks[-1]
        tk["wpart_free"] = y_done
        for nm, src in (("dbg_wqT", rdv(0, BF16, 8 * 1024)),
                        ("dbg_wkT", rdv(16384, BF16, 8 * 1024)),
                        ("dbg_qtm_r", rdv(32768, BF16, 1024)),
                        ("dbg_ktm_r", rdv(32768 + 2048, BF16, 1024)),
                        ("dbg_vtm_r", rdv(32768 + 4096, BF16, 1024)),
                        ("dbg_asb", rdv(32768 + 10240, BF16, 16 * 128)),
                        ("dbg_qfmc", rdv(32768 + 6144, BF16, 1024)),
                        ("dbg_kfmc", rdv(32768 + 8192, BF16, 1024))):
            if nm in dbg_out and s == 0:
                dbg_dump(nm, src, [y_done])
        if "dbg_ytm" in dbg_out and s == 0:
            dbg_dump("dbg_ytm", ar.view(TM16.off, BF16, NT * 1024), [y_done])

        if stop_phase <= 3:
            return early_out(s, y_done)
        yfm_toks = []
        for c in range(8):
            for t in range(NT):
                bk = banks[(t % 2)]
                tt = transpose(bk, TM16.ap()[:, t, c * 128:(c + 1) * 128],
                               deps=y_toks[2 * t:2 * t + 2])
                yfm_toks.append(evac(bk, FMB.ap()[:, c, t * 128:(t + 1) * 128]))
        yfm_done = yfm_toks[-1]

        def YFM(c):
            return FMB.ap()[:, c, 0:T]

        gn_toks = []
        scrb = banks[2]
        nblk = T // 512
        for c in range(8):
            dep = [yfm_done]
            if scrb.rtok is not None:
                dep.append(scrb.rtok)
            last = dep[0]
            for b in range(nblk):
                a1 = vop(lambda e, c=c, b=b: e.reduce_sum(
                    sm(32 + b), YFM(c)[:, b * 512:(b + 1) * 512], axis=AX.X), deps=dep)
                a2a = aop(lambda e, c=c, b=b: e.activation(
                    scrb.f32(0, 512), YFM(c)[:, b * 512:(b + 1) * 512], AF.Square),
                    deps=[a1])
                a2 = vop(lambda e, b=b: e.reduce_sum(
                    sm(36 + b), scrb.f32(0, 512), axis=AX.X), deps=[a2a])
                dep = [a2]
                last = a2
            scrb.wtok = last
            scrb.rtok = last
            for (dst, src0) in ((16 + c, 32), (24 + c, 36)):
                tkg = vop(lambda e, dst=dst, src0=src0: e.tensor_copy(sm(dst), sm(src0)),
                         deps=[last])
                for b in range(1, nblk):
                    tkg = vop(lambda e, dst=dst, s2=src0 + b: e.tensor_tensor(
                        sm(dst), sm(dst), sm(s2), op=ALU.add), deps=[tkg])
                gn_toks.append(tkg)
        halfsel = cv("halfsel", F32, 2)
        smrow = cv("smalls", F32, 64)
        rs_sq = smrow[:, 16:32]
        hm0 = mm_ap(banks[5], banks[5].t.ap()[0:1, 0:16], halfsel[:, 0:1], rs_sq,
                    True, True, deps=gn_toks)
        hm1 = mm_ap(banks[5], banks[5].t.ap()[0:1, 16:32], halfsel[:, 1:2], rs_sq,
                    True, True, deps=gn_toks)
        banks[5].wtok = hm1
        ngrp = float(T * 64)
        b5 = banks[5].t.ap()
        bc_toks = []
        for h in range(2):
            me, tv, t2, rs, mr = 8 * h, 16 + 8 * h, 24 + 8 * h, 32 + 8 * h, 48 + 8 * h
            k1 = vop(lambda e, h=h, me=me: e.tensor_scalar_mul(
                smrow[0:1, me:me + 8], b5[0:1, 16 * h:16 * h + 8], 1.0 / ngrp),
                deps=[hm1])
            k2 = vop(lambda e, h=h, tv=tv: e.tensor_scalar_mul(
                smrow[0:1, tv:tv + 8], b5[0:1, 16 * h + 8:16 * h + 16], 1.0 / ngrp),
                deps=[k1])
            banks[5].rtok = k2
            k3 = vop(lambda e, me=me, t2=t2: e.tensor_mul(
                smrow[0:1, t2:t2 + 8], smrow[0:1, me:me + 8], smrow[0:1, me:me + 8]),
                deps=[k2])
            k4 = vop(lambda e, tv=tv, t2=t2: e.tensor_sub(
                smrow[0:1, tv:tv + 8], smrow[0:1, tv:tv + 8], smrow[0:1, t2:t2 + 8]),
                deps=[k3])
            k5 = aop(lambda e, tv=tv, rs=rs: e.activation(
                smrow[0:1, rs:rs + 8], smrow[0:1, tv:tv + 8], AF.Ln, bias=EPSC[0:1, :]),
                deps=[k4])
            k6 = aop(lambda e, rs=rs: e.activation(
                smrow[0:1, rs:rs + 8], smrow[0:1, rs:rs + 8], AF.Exp, scale=-0.5),
                deps=[k5])
            k7 = vop(lambda e, me=me, rs=rs, mr=mr: e.tensor_mul(
                smrow[0:1, mr:mr + 8], smrow[0:1, me:me + 8], smrow[0:1, rs:rs + 8]),
                deps=[k6])
            m1 = mm_ap(banks[4], banks[4].t.ap()[:, 8 * h:8 * h + 8], ONES_F32,
                       smrow[0:1, rs:rs + 8], h == 0, False, deps=[k7])
            m2 = mm_ap(banks[4], banks[4].t.ap()[:, 16 + 8 * h:24 + 8 * h], ONES_F32,
                       smrow[0:1, mr:mr + 8], False, h == 1, deps=[k7],
                       inc=(h == 1))
            bc_toks.append(m2)
        bctok = bc_toks[-1]
        banks[4].wtok = bctok
        Sf = sv(0, F32, 8)
        Bf = sv(64, F32, 8)
        TmpF = sv(128, F32, 8)
        gnw = cv("gnw", F32, 8)
        gnb = cv("gnb", F32, 8)
        b4 = banks[4].t.ap()
        f1 = vop(lambda e: e.tensor_mul(Sf[0:64, :], b4[0:64, 0:8], gnw[0:64, :]),
                 deps=[bctok])
        f2 = vop(lambda e: e.tensor_mul(Sf[64:128, :], b4[64:128, 8:16], gnw[64:128, :]),
                 deps=[f1])
        f3 = vop(lambda e: e.tensor_mul(TmpF[0:64, :], b4[0:64, 16:24], gnw[0:64, :]),
                 deps=[f2])
        f4 = vop(lambda e: e.tensor_mul(TmpF[64:128, :], b4[64:128, 24:32], gnw[64:128, :]),
                 deps=[f3])
        banks[4].rtok = f4
        f5 = vop(lambda e: e.tensor_sub(Bf, gnb, TmpF), deps=[f4])
        yn_toks = []
        for c in range(8):
            yn_toks.append(aop(lambda e, c=c: e.activation(
                YFM(c), YFM(c), AF.Identity, bias=Bf[:, c:c + 1], scale=Sf[:, c:c + 1]),
                deps=[f5] + gn_toks))
        yn_done = yn_toks[-1]
        if "dbg_ynfm" in dbg_out and s == 0:
            dbg_dump("dbg_ynfm", FMB.ap()[:, :, 0:T], [yn_done])

        woT = rdv(0, BF16, 8, 1024)
        if s > 0 and tk.get("wtT_o"):
            woT_done = p.dma("sync", woT, wtT_scr["o"],
                             deps=[yfm_done, tk["wtT_o"]])
        else:
            stage2 = tmv(0, BF16, 8, 1024)
            sw = p.dma("sync", stage2, wattn_scr["o"],
                       deps=[yfm_done, cast_tok["wo"]])
            wo_toks = []
            for cf in range(8):
                for cd in range(8):
                    bk = banks[6 + (cd % 2)]
                    transpose(bk, stage2[:, cf, cd * 128:(cd + 1) * 128], deps=[sw])
                    wo_toks.append(evac(bk, woT[:, cd, cf * 128:(cf + 1) * 128]))
            woT_done = wo_toks[-1]
            tk["wtT_o"] = p.dma("gpsimd", wtT_scr["o"], woT, deps=[woT_done])

        ln2_war = [None, None]
        attn_tile_tok = []
        for t in range(NT):
            for cd in range(8):
                lhsT = FMB.ap()[:, cd, t * 128:(t + 1) * 128]
                mm(banks[6], 0, 512, lhsT, woT[:, cd, 0:512], cd == 0, cd == 7,
                   deps=[yn_done, woT_done])
                mm(banks[7], 0, 512, lhsT, woT[:, cd, 512:1024], cd == 0, cd == 7,
                   deps=[yn_done, woT_done])
            stok, rstd, negmr = ln_stats2([banks[6].f32(0, 512), banks[7].f32(0, 512)],
                                          banks[5], 0 if t % 2 == 0 else 8,
                                          deps=[banks[6].wtok, banks[7].wtok],
                                          war_dep=ln2_war[t % 2], from_psum=True)
            r1 = vop(lambda e, t=t, n=negmr: e.tensor_scalar_add(
                XO.ap()[:, t, :], XO.ap()[:, t, :], n), deps=[stok])
            r2 = vop(lambda e, t=t, r=rstd: e.scalar_tensor_tensor(
                XO.ap()[:, t, 0:512], banks[6].f32(0, 512), r,
                XO.ap()[:, t, 0:512], op0=ALU.mult, op1=ALU.add), deps=[r1])
            banks[6].rtok = r2
            r3 = vop(lambda e, t=t, r=rstd: e.scalar_tensor_tensor(
                XO.ap()[:, t, 512:1024], banks[7].f32(0, 512), r,
                XO.ap()[:, t, 512:1024], op0=ALU.mult, op1=ALU.add), deps=[r2])
            banks[7].rtok = r3
            ln2_war[t % 2] = r3
            attn_tile_tok.append(r3)
            if paranoid:
                p.barrier()
        attn_done = r3
        if "dbg_xattn" in dbg_out:
            dbg_dump_src = XO.ap()
            tok_ = p.dma("gpsimd",
                         dbg_out["dbg_xattn"][s].rearrange("(nt p) d -> p nt d", p=128),
                         dbg_dump_src, deps=[attn_done])
            for eng_ in ENGS:
                p.op(eng_, lambda e: None, deps=[tok_], inc=False)

        if stop_phase <= 4:
            return early_out(s, attn_done)
        XS = XO
        ln3_toks = []
        ln3_war = [None, None]
        spill_view = xo_spill_d.rearrange("(nt p) d -> p nt d", p=128)
        for t in range(NT):
            xt = XS.ap()[:, t, :]
            spill_t = p.dma("gpsimd", spill_view[:, t, :], xt,
                            deps=[attn_done])
            stok, rstd, negmr = ln_stats2([xt[:, 0:512], xt[:, 512:1024]], banks[4],
                                          48 if t % 2 == 0 else 56,
                                          deps=[spill_t],
                                          war_dep=ln3_war[t % 2])
            tk3 = aop(lambda e, xt=xt, r=rstd, n=negmr: e.activation(
                xt, xt, AF.Identity, bias=n, scale=r), deps=[stok, spill_t])
            ln3_toks.append(tk3)
            ln3_war[t % 2] = tk3
        xs_tok = ln3_toks[-1]
        if "dbg_xs1" in dbg_out and s == 0:
            tok_ = p.dma("sync", dbg_out["dbg_xs1"].rearrange("(nt p) d -> p nt d", p=128),
                         XS.ap(), deps=[xs_tok])
            for eng_ in ENGS:
                p.op(eng_, lambda e: None, deps=[tok_], inc=False)

        if stop_phase <= 5:
            return early_out(s, xs_tok)
        for l in range(2):
            xs_tok = run_ffn_layer(s, l, xs_tok)
            if stop_phase <= 6 + l:
                return early_out(s, xs_tok)

        xob = sv(0, F32, 2, 1024)
        out_toks = []
        xob_rtok = [None, None]
        f2_war = [None, None]
        for t in range(NT):
            xt = XS.ap()[:, t, :]
            stok, rstd, negmr = ln_stats2([xt[:, 0:512], xt[:, 512:1024]],
                                          banks[4 + ((t + 1) % 2)],
                                          0 if t % 2 == 0 else 8, deps=[xs_tok],
                                          war_dep=f2_war[t % 2])
            dl = [xs_tok]
            if xob_rtok[t % 2] is not None:
                dl.append(xob_rtok[t % 2])
            ld = p.dma("sync", xob[:, t % 2, :],
                       xo_spill_d.rearrange("(nt p) d -> p nt d", p=128)[:, t, :],
                       deps=dl)
            if t == 0 and "dbg_xob_ld" in dbg_out and s == 0:
                dbg_dump("dbg_xob_ld", xob[:, 0, :], [ld, stok])
            o1 = vop(lambda e, t=t, r=rstd, xt=xt: e.scalar_tensor_tensor(
                xob[:, t % 2, :], xt, r, xob[:, t % 2, :],
                op0=ALU.mult, op1=ALU.add), deps=[stok, ld])
            if t == 0 and "dbg_xob_o1" in dbg_out and s == 0:
                dbg_dump("dbg_xob_o1", xob[:, 0, :], [o1])
                dbg_dump("dbg_f2rstd", cv("smalls", F32, 64)[:, 6:8], [stok])
            o2 = vop(lambda e, t=t, n=negmr: e.tensor_scalar_add(
                xob[:, t % 2, :], xob[:, t % 2, :], n), deps=[o1])
            ot = p.dma("sync", y_out[s].rearrange("(nt p) d -> p nt d", p=128)[:, t, :],
                       xob[:, t % 2, :], deps=[o2])
            xob_rtok[t % 2] = ot
            f2_war[t % 2] = o2
            out_toks.append(ot)
            if paranoid:
                p.barrier()
        return out_toks

    prev = [ctok]
    for s in range(nseq):
        if s > 0:
            p.barrier()
        prev = run_seq(s, prev)

    p.raw("sync", lambda e, v=p.cnt["dsp"], sem=p.sems["dsp"]: e.wait_ge(sem, v))
    p.raw("sync", lambda e, v=p.cnt["dact"], sem=p.sems["dact"]: e.wait_ge(sem, v))
    p.raw("sync", lambda e, v=p.cnt["dpool"], sem=p.sems["dpool"]: e.wait_ge(sem, v))

    with nc.Block() as block:
        p.finish(block)
    return nc


_NC_CACHE = {}


def kernel(**inputs):
    T = 2048
    nseq = 2
    consts = build_consts(T)
    lay = layout_inputs(inputs, T)
    key = (T, nseq)
    if key not in _NC_CACHE:
        _NC_CACHE[key] = build_core(T=T, nseq=nseq)
    nc = _NC_CACHE[key]

    x = np.ascontiguousarray(inputs["L0_attn_x"], dtype=np.float32)
    common = {}
    for nm in ("L0_attn_W_q", "L0_attn_W_k", "L0_attn_W_v", "L0_attn_W_o",
               "L0_ffn_wqkv", "L0_ffn_wo", "L0_ffn_w1", "L0_ffn_w2"):
        common[nm] = np.ascontiguousarray(inputs[nm], dtype=np.float32)
    for nm in ("identity", "ones_mat", "ones_row_f32", "gq_tbl", "gk_tbl", "gC1_tbl",
               "gC_tbl", "g1_tbl", "mask_keep", "amask_lo", "amask_hi",
               "rot_cq", "rot_sq", "rot_ck", "rot_sk", "halfsel"):
        common[nm] = consts[nm]
    for nm in ("gnw_fm", "gnb_fm", "lg_fm", "lb_fm", "b2_fm", "b1a", "b1g",
               "qs_rep", "ks_rep"):
        common[nm] = lay[nm]

    in_maps = []
    for c in range(NCORES):
        m = dict(common)
        m["x"] = x[c * nseq:(c + 1) * nseq]
        in_maps.append(m)
    res = run_bass_kernel_spmd(nc, in_maps, core_ids=list(range(NCORES)))
    out = np.concatenate([np.asarray(r["y"], dtype=np.float32) for r in res.results],
                         axis=0)
    return out


# version 38
# speedup vs baseline: 1.0144x; 1.0015x over previous
import math
import numpy as np
import ml_dtypes
import concourse.bass as bass
import concourse.mybir as mybir
from concourse.bass_utils import run_bass_kernel_spmd

F32 = mybir.dt.float32
BF16 = mybir.dt.bfloat16
U8 = mybir.dt.uint8
AF = mybir.ActivationFunctionType
ALU = mybir.AluOpType
AX = mybir.AxisListType

BF = ml_dtypes.bfloat16

D = 1024
NH_R = 16
DH_R = 64
NH_F = 8
DH_F = 64
INNER = 512
HID = 2730
NJ = 22
LAST_J = HID - 21 * 128
WINDOW = 512
QK_SCALE = 8.0
EPS = 1e-5
NCORES = 8


def build_consts(T):
    NT = T // 128
    c = {}
    c["identity"] = np.eye(128, dtype=np.float32).astype(BF)
    c["ones_row"] = np.ones((1, 128), dtype=BF)
    c["ones_row_f32"] = np.ones((1, 128), dtype=np.float32)
    c["ones_mat"] = np.ones((128, 128), dtype=BF)
    hs = np.zeros((128, 2), dtype=np.float32)
    hs[:64, 0] = 1.0
    hs[64:, 1] = 1.0
    c["halfsel"] = hs

    gam = 1.0 - 2.0 ** (-5.0 - np.arange(NH_R, dtype=np.float64))
    p = np.arange(128, dtype=np.float64)[:, None]
    c["gq_tbl"] = (gam[None, :] ** p / 8.0).astype(np.float32)
    c["gk_tbl"] = (gam[None, :] ** (-p)).astype(np.float32)
    blk = np.arange(8, dtype=np.int64)
    half = (np.arange(128) >= 64).astype(np.int64)
    headmap = 2 * blk[None, :] + half[:, None]
    gh = gam[headmap]
    c["gC1_tbl"] = (gh ** 127).astype(np.float32)
    c["gC_tbl"] = (gh ** 128).astype(np.float32)
    c["g1_tbl"] = gh.astype(np.float32)
    tp = np.arange(128)
    c["mask_keep"] = (tp[:, None] <= tp[None, :]).astype(np.float32)

    NEG = -1e30
    c["amask_lo"] = np.where(tp[None, :] < tp[:, None], NEG, 0.0).astype(np.float32)
    c["amask_hi"] = np.where(tp[None, :] > tp[:, None], NEG, 0.0).astype(np.float32)

    half_idx = np.arange(0, DH_F, 2, dtype=np.float64)
    inv_freq = 1.0 / 10000.0 ** (half_idx / DH_F)
    t = np.arange(T, dtype=np.float64)
    freqs = t[:, None] * inv_freq[None, :]
    freqs = np.concatenate([freqs, freqs], axis=-1)
    base = (half_idx + 0.4 * DH_F) / (1.4 * DH_F)
    power = (t - T // 2) / float(WINDOW // 2)
    scale = base[None, :] ** power[:, None]
    scale = np.concatenate([scale, scale], axis=-1)
    cos = np.cos(freqs)
    sin = np.sin(freqs)
    sign = np.where(np.arange(DH_F) < 32, -1.0, 1.0)[None, :]
    cq = cos * scale
    sq = sin * scale * sign
    ck = cos / scale
    sk = sin / scale * sign
    for nm, tab in (("rot_cq", cq), ("rot_sq", sq), ("rot_ck", ck), ("rot_sk", sk)):
        c[nm] = np.tile(tab.astype(BF), (1, NH_F))
    return c


def layout_inputs(inputs, T):
    d = {}
    d["gnw_fm"] = np.ascontiguousarray(inputs["L0_attn_gn_weight"].reshape(8, 128).T)
    d["gnb_fm"] = np.ascontiguousarray(inputs["L0_attn_gn_bias"].reshape(8, 128).T)
    d["lg_fm"] = np.ascontiguousarray(inputs["L0_ffn_ln_g"].reshape(2, 8, 128).transpose(0, 2, 1))
    d["lb_fm"] = np.ascontiguousarray(inputs["L0_ffn_ln_b"].reshape(2, 8, 128).transpose(0, 2, 1))
    d["b2_fm"] = np.ascontiguousarray(inputs["L0_ffn_b2"].reshape(2, 8, 128).transpose(0, 2, 1))
    b1 = inputs["L0_ffn_b1"]
    b1a = np.zeros((2, 128, NJ), dtype=np.float32)
    b1g = np.zeros((2, 128, NJ), dtype=np.float32)
    for j in range(NJ):
        n = 128 if j < NJ - 1 else LAST_J
        b1a[:, :n, j] = b1[:, j * 128:j * 128 + n]
        b1g[:, :n, j] = b1[:, HID + j * 128:HID + j * 128 + n]
    d["b1a"] = b1a
    d["b1g"] = b1g
    qs = inputs["L0_ffn_q_scale"]
    ks = inputs["L0_ffn_k_scale"]
    d["qs_rep"] = np.ascontiguousarray(
        np.broadcast_to(np.tile(qs, (1, NH_F))[:, None, :], (2, 128, 512))).astype(np.float32)
    d["ks_rep"] = np.ascontiguousarray(
        np.broadcast_to(np.tile(ks, (1, NH_F))[:, None, :], (2, 128, 512))).astype(np.float32)
    return d


ENGS = ("tensor", "vector", "scalar", "gpsimd", "sync")


class Prog:

    def __init__(self, nc):
        self.nc = nc
        self.queues = {e: [] for e in ENGS}
        self.sems = {}
        self.cnt = {}
        self.waited = {e: {} for e in ENGS}
        self._ctxs = []
        for key in ("pe", "dve", "act", "pool", "sp", "dsp", "dact", "dpool"):
            cm = nc.semaphore(f"s_{key}")
            self.sems[key] = cm.__enter__()
            self._ctxs.append(cm)
            self.cnt[key] = 0

    ENG2SEM = {"tensor": "pe", "vector": "dve", "scalar": "act", "gpsimd": "pool", "sync": "sp"}
    DMAQ2SEM = {"sync": "dsp", "scalar": "dact", "gpsimd": "dpool"}

    def _waits(self, eng, deps):
        out = []
        w = self.waited[eng]
        for (sk, val) in deps:
            if val is None:
                continue
            if w.get(sk, 0) < val:
                out.append((self.sems[sk], val))
                w[sk] = val
        return out

    def op(self, eng, fn, deps=(), inc=True):
        waits = self._waits(eng, deps)
        semkey = self.ENG2SEM[eng]
        tok = None
        if inc:
            self.cnt[semkey] += 1
            tok = (semkey, self.cnt[semkey])
            if eng == "tensor":
                self.waited[eng][semkey] = self.cnt[semkey]
        sem = self.sems[semkey]

        def run(e, waits=waits, fn=fn, inc=inc, sem=sem):
            for (s, v) in waits:
                e.wait_ge(s, v)
            inst = fn(e)
            if inc:
                assert inst is not None, "inc=True op returned no instruction"
                inst.then_inc(sem, 1)
        self.queues[eng].append(run)
        return tok

    def dma(self, queue, out_ap, in_ap, deps=(), slow=False):
        eng = queue
        waits = self._waits(eng, deps)
        semkey = self.DMAQ2SEM[queue]
        self.cnt[semkey] += 16
        tok = (semkey, self.cnt[semkey])
        sem = self.sems[semkey]

        def run(e, waits=waits, out_ap=out_ap, in_ap=in_ap, sem=sem, slow=slow):
            for (s, v) in waits:
                e.wait_ge(s, v)
            if slow:
                e.dma_start(out_ap, in_ap, allow_slow_non_contiguous=True).then_inc(sem, 16)
            else:
                e.dma_start(out_ap, in_ap).then_inc(sem, 16)
        self.queues[eng].append(run)
        return tok

    def raw(self, eng, fn):
        self.queues[eng].append(fn)

    def barrier(self):
        targets = [(k, v) for k, v in self.cnt.items() if v > 0]
        for eng in ENGS:
            waits = self._waits(eng, targets)
            if not waits:
                continue

            def run(e, waits=waits):
                for (sem, v) in waits:
                    e.wait_ge(sem, v)
            self.queues[eng].append(run)

    def finish(self, block):
        for eng in ENGS:
            q = self.queues[eng]
            getattr(block, eng)(lambda e, q=q: [f(e) for f in q])


class Arena:

    def __init__(self, nc, nbytes, name="arena"):
        self.nbytes = nbytes
        self.cm = nc.sbuf_tensor(name, [128, nbytes], U8)
        self.t = self.cm.__enter__()
        self.cur = 0
        self.marks = []
        self.live = {}

    def alloc(self, name, nbytes):
        nbytes = (nbytes + 63) & ~63
        off = self.cur
        assert off + nbytes <= self.nbytes, (
            f"arena overflow: {name} needs {nbytes} at {off}, cap {self.nbytes}")
        self.cur += nbytes
        self.live[name] = (off, nbytes)
        return off

    def push(self):
        self.marks.append(self.cur)

    def pop(self):
        self.cur = self.marks.pop()

    def view(self, off, dtype, *shape):
        esz = {F32: 4, BF16: 2}[dtype]
        n = 1
        for s in shape:
            n *= s
        ap = self.t.ap()[:, off:off + n * esz].bitcast(dtype)
        if len(shape) > 1:
            kw = {}
            names = []
            for i, s in enumerate(shape):
                names.append(f"d{i}")
                kw[f"d{i}"] = s
            ap = ap.rearrange(f"p ({' '.join(names)}) -> p {' '.join(names)}", **kw)
        return ap


class Buf:

    def __init__(self, arena, name, dtype, shape):
        esz = {F32: 4, BF16: 2}[dtype]
        n = 1
        for s in shape:
            n *= s
        self.arena = arena
        self.off = arena.alloc(name, n * esz)
        self.dtype = dtype
        self.shape = shape

    def ap(self, *shape):
        return self.arena.view(self.off, self.dtype, *(shape if shape else self.shape))


class Bank:

    def __init__(self, nc, i):
        self.cm = nc.psum_tensor(f"pb{i}", [128, 512], F32)
        self.t = self.cm.__enter__()
        self.wtok = None
        self.rtok = None

    def f32(self, lo=0, hi=512):
        return self.t.ap()[:, lo:hi]

    def f32v(self, lo, *shape):
        ap = self.t.ap()
        n = 1
        for s in shape:
            n *= s
        ap = ap[:, lo:lo + n]
        if len(shape) > 1:
            kw = {f"d{i}": s for i, s in enumerate(shape)}
            ap = ap.rearrange(f"p ({' '.join(kw)}) -> p {' '.join(kw)}", **kw)
        return ap

    def bf16(self, lo=0, hi=512):
        return self.t.ap()[:, lo:hi].bitcast(BF16)


def build_core(T=2048, nseq=2, dbg=(), stop_phase=99, retpart='full', paranoid=0):
    NT = T // 128
    nc = bass.Bass("TRN2", target_bir_lowering=False, detect_race_conditions=False)

    def din(name, shape, dt=F32):
        return nc.dram_tensor(name, list(shape), dt, kind="ExternalInput").ap()

    x_in = din("x", [nseq, T, D])
    y_out = nc.dram_tensor("y", [nseq, T, D], F32, kind="ExternalOutput").ap()

    wq_d = din("L0_attn_W_q", [D, D])
    wk_d = din("L0_attn_W_k", [D, D])
    wv_d = din("L0_attn_W_v", [D, D])
    wo_d = din("L0_attn_W_o", [D, D])
    wqkv_d = din("L0_ffn_wqkv", [2, D, 3 * INNER])
    fwo_d = din("L0_ffn_wo", [2, INNER, D])
    w1_d = din("L0_ffn_w1", [2, D, 2 * HID])
    w2_d = din("L0_ffn_w2", [2, HID, D])

    identity_d = din("identity", [128, 128], BF16)
    ones_mat_d = din("ones_mat", [128, 128], BF16)
    ones_row_f32_d = din("ones_row_f32", [1, 128])
    gq_d = din("gq_tbl", [128, 16])
    gk_d = din("gk_tbl", [128, 16])
    gC1_d = din("gC1_tbl", [128, 8])
    gC_d = din("gC_tbl", [128, 8])
    g1_d = din("g1_tbl", [128, 8])
    maskk_d = din("mask_keep", [128, 128])
    amlo_d = din("amask_lo", [128, 128])
    amhi_d = din("amask_hi", [128, 128])
    rot_d = {nm: din(nm, [T, 512], BF16) for nm in ("rot_cq", "rot_sq", "rot_ck", "rot_sk")}
    gnw_d = din("gnw_fm", [128, 8])
    gnb_d = din("gnb_fm", [128, 8])
    lg_d = din("lg_fm", [2, 128, 8])
    lb_d = din("lb_fm", [2, 128, 8])
    b2_d = din("b2_fm", [2, 128, 8])
    b1a_d = din("b1a", [2, 128, NJ])
    b1g_d = din("b1g", [2, 128, NJ])
    qs_d = din("qs_rep", [2, 128, 512])
    ks_d = din("ks_rep", [2, 128, 512])
    halfsel_d = din("halfsel", [128, 2])

    xo_spill_d = nc.dram_tensor("xo_spill", [T, D], F32, kind="Internal").ap()

    wattn_scr = {nm: nc.dram_tensor(f"w{nm}_scr", [128, 8, 1024], BF16,
                                    kind="Internal").ap()
                 for nm in ("q", "k", "v", "o")}
    wqkv_scr = nc.dram_tensor("wqkv_scr", [2, 128, 8, 3 * INNER], BF16,
                              kind="Internal").ap()
    fwo_scr = nc.dram_tensor("fwo_scr", [2, 128, 4, D], BF16, kind="Internal").ap()
    w1_scr = nc.dram_tensor("w1_scr", [2, 128, 8, 2 * HID], BF16,
                            kind="Internal").ap()
    w2_scr = nc.dram_tensor("w2_scr", [2, 128, NJ, D], BF16, kind="Internal").ap()
    wtT_scr = {nm: nc.dram_tensor(f"wtT_{nm}_scr", [128, 8, 1024], BF16,
                                  kind="Internal").ap()
               for nm in ("q", "k", "v", "o")}

    dbg_shapes = dict(
        dbg_lnxfm=[128, 8 * (T + 4)], dbg_ytm=[128, NT * 1024],
        dbg_qtm_r=[128, 1024], dbg_ktm_r=[128, 1024], dbg_vtm_r=[128, 1024],
        dbg_wqT=[128, 8 * 1024], dbg_wkT=[128, 8 * 1024],
        dbg_wqT_early=[128, 8 * 1024], dbg_idn=[128, 128],
        dbg_qraw0=[128, 512], dbg_rstd0=[128, 8], dbg_q10=[128, 512],
        dbg_qfm_early=[128, 4 * T], dbg_qfm_t0a=[128, 4 * 128], dbg_qfm_t0b=[128, 4 * 128],
        dbg_xob_ld=[128, 1024], dbg_xob_o1=[128, 1024], dbg_f2rstd=[128, 2],
        dbg_smalls=[128, 64], dbg_bounce=[128, 1024],
        dbg_t10=[128, 512], dbg_t20=[128, 512], dbg_qft0=[128, 512],
        dbg_asb=[128, 16 * 128], dbg_qfmc=[128, 1024], dbg_kfmc=[128, 1024],
        dbg_xattn=[nseq, T, D], dbg_qfm=[128, 4 * T], dbg_kfm=[128, 4 * T],
        dbg_vtm=[128, NT * 512], dbg_ofm=[128, 4 * T], dbg_xs1=[T, D],
        dbg_h2=[128, 8 * T], dbg_xsfin=[T, D], dbg_ynfm=[128, 8 * T],
    )
    dbg_out = {}
    for name in dbg:
        dbg_out[name] = nc.dram_tensor(
            name, dbg_shapes[name], F32, kind="ExternalOutput").ap()

    ar = Arena(nc, 206 * 1024)
    banks = [Bank(nc, i) for i in range(8)]
    p = Prog(nc)

    FMW = T + 4
    XO = Buf(ar, "XO", F32, (NT, 1024))
    FMB = Buf(ar, "FM", BF16, (8, FMW))
    TM16 = Buf(ar, "TM16", BF16, (max(NT, 16), 1024))
    RD = Buf(ar, "RD", BF16, (24, 1024))
    SCR = Buf(ar, "SCR", F32, (6 * 1024,))
    CST = Buf(ar, "CST", F32, (1024,))

    co = {}
    _c = [0]

    def calloc(name, n):
        co[name] = _c[0]
        _c[0] += n
    for nm, n in (("gq", 16), ("gk", 16), ("gC1", 8), ("gC", 8), ("g1", 8),
                  ("maskk", 128), ("amlo", 128), ("amhi", 128),
                  ("halfsel", 2), ("gnw", 8), ("gnb", 8),
                  ("identity", 64), ("ones_mat", 64), ("ones_row_f32", 128),
                  ("lg", 8), ("lb", 8), ("b2", 8), ("b1a", NJ), ("b1g", NJ),
                  ("epscol", 1), ("ln8col", 1), ("onecol", 1), ("n2eps", 1),
                  ("lnn", 1), ("tinyc", 1), ("smalls", 64)):
        calloc(nm, n)
    assert _c[0] <= 1024, _c[0]

    def cv(name, dtype, *shape):
        return ar.view(CST.off + co[name] * 4, dtype, *shape)

    def sv(byte_off, dtype, *shape):
        return ar.view(SCR.off + byte_off, dtype, *shape)

    def rdv(byte_off, dtype, *shape):
        return ar.view(RD.off + byte_off, dtype, *shape)

    def tmv(byte_off, dtype, *shape):
        return ar.view(TM16.off + byte_off, dtype, *shape)

    def sm(i):
        return cv("smalls", F32, 64)[:, i:i + 1]

    IDN = cv("identity", BF16, 128)
    ONES_F32 = cv("ones_row_f32", F32, 128)[0:1, :]
    EPSC = cv("epscol", F32, 1)
    N2EPS = cv("n2eps", F32, 1)
    TINYC = cv("tinyc", F32, 1)
    LNN = cv("lnn", F32, 1)
    LN8C = cv("ln8col", F32, 1)
    ONEC = cv("onecol", F32, 1)

    p.op("vector", lambda e: e.memset(cv("epscol", F32, 1), EPS))
    p.op("vector", lambda e: e.memset(cv("n2eps", F32, 1), 1024.0 * 1024.0 * EPS))
    p.op("vector", lambda e: e.memset(cv("tinyc", F32, 1), 1e-24))
    p.op("vector", lambda e: e.memset(cv("lnn", F32, 1), float(math.log(1024.0))))
    p.op("vector", lambda e: e.memset(cv("onecol", F32, 1), 1.0))
    p.op("vector", lambda e: e.memset(cv("ln8col", F32, 1), float(math.log(QK_SCALE))))
    ctoks = []
    for name, dram in (("gq", gq_d), ("gk", gk_d), ("gC1", gC1_d), ("gC", gC_d),
                       ("g1", g1_d), ("maskk", maskk_d), ("amlo", amlo_d),
                       ("amhi", amhi_d), ("halfsel", halfsel_d),
                       ("gnw", gnw_d), ("gnb", gnb_d),
                       ("ones_row_f32", ones_row_f32_d)):
        dst = cv(name, F32, *dram.shape[1:])
        if dram.shape[0] != 128:
            dst = dst[0:dram.shape[0], :]
        ctoks.append(p.dma("sync", dst, dram))
    ctoks.append(p.dma("sync", IDN, identity_d))
    ctoks.append(p.dma("sync", cv("ones_mat", BF16, 128), ones_mat_d))
    ONES_MAT = cv("ones_mat", BF16, 128)
    ctok = ctoks[-1]

    def mm(bank, lo, hi, lhsT, rhs, start, stop, deps=(), inc=None, tile_position=None,
           sg=False):
        d = list(deps)
        if start and bank.rtok is not None:
            d.append(bank.rtok)
        do_inc = stop if inc is None else inc
        tok = p.op("tensor",
                   lambda e, b=bank, lo=lo, hi=hi, l=lhsT, r=rhs, s=start, t=stop,
                   tp=tile_position, sg=sg:
                   e.matmul(b.f32(lo, hi), l, r, start=s, stop=t, tile_position=tp,
                            skip_group_check=sg),
                   deps=d, inc=do_inc)
        if do_inc:
            bank.wtok = tok
        return tok

    def mm_ap(bank, out_ap, lhsT, rhs, start, stop, deps=(), inc=None,
              tile_position=None, skip_group_check=False):
        d = list(deps)
        if start and bank.rtok is not None:
            d.append(bank.rtok)
        do_inc = stop if inc is None else inc
        tok = p.op("tensor",
                   lambda e, o=out_ap, l=lhsT, r=rhs, s=start, t=stop, tp=tile_position,
                   sg=skip_group_check:
                   e.matmul(o, l, r, start=s, stop=t, tile_position=tp,
                            skip_group_check=sg),
                   deps=d, inc=do_inc)
        if do_inc:
            bank.wtok = tok
        return tok

    def transpose(bank, src_ap, deps=()):
        d = list(deps) + [ctok]
        if bank.rtok is not None:
            d.append(bank.rtok)
        tok = p.op("tensor",
                   lambda e, b=bank, s=src_ap:
                   e.transpose(b.t.ap()[:, 0:64].bitcast(BF16), s, IDN),
                   deps=d, inc=True)
        bank.wtok = tok
        return tok

    def evac(bank, dst_ap, deps=(), src_lo=0, src_n=128, bf=True):
        d = list(deps) + [bank.wtok]
        tok = p.op("vector",
                   lambda e, b=bank, dst=dst_ap, lo=src_lo, n=src_n:
                   e.tensor_copy(dst, b.t.ap()[:, lo // 2:(lo + n) // 2].bitcast(BF16)),
                   deps=d, inc=True)
        bank.rtok = tok
        return tok

    def vop(fn, deps=(), inc=True):
        return p.op("vector", fn, deps=deps, inc=inc)

    def dbg_dump(name, src, deps):
        tok = p.dma("gpsimd", dbg_out[name], src, deps=deps)
        for eng in ENGS:
            p.op(eng, lambda e: None, deps=[tok], inc=False)


    def aop(fn, deps=(), inc=True):
        return p.op("scalar", fn, deps=deps, inc=inc)

    def bc3(ap2d, n_inner):
        return ap2d[:, :, None].broadcast_to([128, ap2d.shape[1], n_inner])


    def ln_stats2(x_parts, scr_bank, base, deps, eps=EPS, n_feat=1024, war_dep=None,
                  from_psum=False):
        ss0, ss1, sq0, sq1, s_me, s_va, s_rs, s_nm = range(base, base + 8)
        d = list(deps)
        if war_dep is not None:
            d.append(war_dep)
        if scr_bank.rtok is not None:
            d.append(scr_bank.rtok)
        p0, p1 = x_parts
        t1 = vop(lambda e: e.reduce_sum(sm(ss0), p0, axis=AX.X), deps=d)
        t1b = vop(lambda e: e.reduce_sum(sm(ss1), p1, axis=AX.X), deps=[t1])
        q1_ = aop(lambda e: e.activation(scr_bank.f32(0, 512), p0, AF.Square),
                  deps=[t1b])
        t2 = vop(lambda e: e.reduce_sum(sm(sq0), scr_bank.f32(0, 512), axis=AX.X),
                 deps=[q1_])
        q2_ = aop(lambda e: e.activation(scr_bank.f32(0, 512), p1, AF.Square),
                  deps=[t2])
        t2b = vop(lambda e: e.reduce_sum(sm(sq1), scr_bank.f32(0, 512), axis=AX.X),
                  deps=[q2_])
        scr_bank.wtok = t2b
        scr_bank.rtok = t2b
        t3 = vop(lambda e: e.tensor_add(sm(s_me), sm(ss0), sm(ss1)), deps=[t2b])
        t4 = vop(lambda e: e.scalar_tensor_tensor(
            sm(sq0), sm(sq0), 1.0, sm(sq1), op0=ALU.mult, op1=ALU.add), deps=[t3])
        t5 = vop(lambda e: e.tensor_mul(sm(s_va), sm(s_me), sm(s_me)), deps=[t4])
        t5b = vop(lambda e: e.scalar_tensor_tensor(
            sm(s_va), sm(sq0), float(n_feat), sm(s_va),
            op0=ALU.mult, op1=ALU.subtract), deps=[t5])
        t6 = aop(lambda e: e.activation(sm(s_rs), sm(s_va), AF.Ln, bias=N2EPS),
                 deps=[t5b])
        t6b = aop(lambda e: e.activation(sm(s_rs), sm(s_rs), AF.Exp, scale=-0.5,
                                         bias=LNN), deps=[t6])
        t7 = vop(lambda e: e.scalar_tensor_tensor(
            sm(s_nm), sm(s_me), -1.0 / n_feat, sm(s_rs), op0=ALU.mult,
            op1=ALU.mult), deps=[t6b])
        return t7, sm(s_rs), sm(s_nm)


    LN8 = math.log(QK_SCALE)
    tk = {"fmb_reader": None, "h2_reader": None, "wpart_free": None}
    cast_tok = {}

    def xs_fm_build(deps, block_cb=None):
        XS = XO
        z0 = vop(lambda e: e.memset(FMB.ap()[:, :, 0:1], 0.0), deps=deps)
        bounce = sv(16384, BF16, 2, 1024)
        b_rtok = [None, None]
        toks = []
        for t in range(NT):
            dd = list(deps)
            if b_rtok[t % 2] is not None:
                dd.append(b_rtok[t % 2])
            ct = vop(lambda e, t=t: e.tensor_copy(bounce[:, t % 2, :], XS.ap()[:, t, :]),
                     deps=dd)
            last_t = None
            for c in range(8):
                bk = banks[6 + (c % 2)]
                last_t = transpose(bk, bounce[:, t % 2, c * 128:(c + 1) * 128],
                                   deps=[ct, z0])
                toks.append(evac(bk, FMB.ap()[:, c, 1 + t * 128:1 + (t + 1) * 128]))
            b_rtok[t % 2] = last_t
            if block_cb is not None:
                block_cb(t, toks[-1])
            if paranoid:
                p.barrier()
        return toks[-1]

    def run_ffn_layer(s, l, xs_tok):
        XS = XO
        NW = max(T // 1024, 1)
        W_T = T // NW
        lgv = cv("lg", F32, 8)
        lbv = cv("lb", F32, 8)
        b2v = cv("b2", F32, 8)
        b1av = cv("b1a", F32, NJ)
        b1gv = cv("b1g", F32, NJ)
        ct1 = p.dma("sync", lgv, lg_d[l], deps=[xs_tok])
        ct2 = p.dma("sync", lbv, lb_d[l], deps=[xs_tok])
        ct3 = p.dma("sync", b2v, b2_d[l], deps=[xs_tok])
        ct4 = p.dma("sync", b1av, b1a_d[l], deps=[xs_tok])
        ct5 = p.dma("sync", b1gv, b1g_d[l], deps=[xs_tok])
        lconst = ct5

        dd = [xs_tok]
        if tk["fmb_reader"] is not None:
            dd.append(tk["fmb_reader"])
        fm_done = xs_fm_build(dd)

        QFM = tmv(0, BF16, 4, T)
        KFM = tmv(16384, BF16, 4, T)
        V_TM = rdv(0, BF16, NT, 512)
        wpart = sv(0, BF16, 8, 512)
        qs_f = sv(8192, F32, 512)
        qs_bf = sv(10240, BF16, 512)
        q_raw = sv(15360, BF16, 512)
        sq_b = sv(16384, BF16, 512)
        q1b = sv(17408, BF16, 512)
        t1b = sv(18432, BF16, 512)
        t2b_ = sv(19456, BF16, 512)
        qf_pp = sv(20480, BF16, 2, 512)
        smv = cv("smalls", F32, 64)

        wpart_rtok = [None]
        qkv_done = None
        tblbuf = [sv(11264 + i * 2048, BF16, 2, 512) for i in range(2)]
        tbl_rtok = [None, None]
        qf_rtok = [None, None]
        qs_rtok = [None]
        for tgt_i, tgt in enumerate(("q", "k", "v")):
            dd = [tk["wpart_free"]] if (tgt_i == 0 and tk.get("wpart_free")) \
                else [fm_done]
            if wpart_rtok[0] is not None:
                dd.append(wpart_rtok[0])
            wtok = p.dma("sync", wpart,
                         wqkv_scr[l][:, :, tgt_i * 512:(tgt_i + 1) * 512],
                         deps=dd + [cast_tok[f"wqkv{l}"]])
            if tgt == "q":
                qd = [lconst] + ([qs_rtok[0]] if qs_rtok[0] else [])
                qtok = p.dma("sync", qs_f, qs_d[l], deps=qd)
                cast_t = vop(lambda e: e.tensor_copy(qs_bf, qs_f), deps=[qtok])
                tbls = ("rot_cq", "rot_sq")
                expb = LN8
            elif tgt == "k":
                qd = [lconst] + ([qs_rtok[0]] if qs_rtok[0] else [])
                qtok = p.dma("sync", qs_f, ks_d[l], deps=qd)
                cast_t = vop(lambda e: e.tensor_copy(qs_bf, qs_f), deps=[qtok])
                tbls = ("rot_ck", "rot_sk")
                expb = 0.0
            last_tok = None
            for t in range(NT):
                bk = banks[t % 2]
                for cd in range(8):
                    c_off = 1 + t * 128 if cd < 4 else t * 128
                    lhsT = FMB.ap()[:, cd, c_off:c_off + 128]
                    mm(bk, 0, 512, lhsT, wpart[:, cd, :], cd == 0, cd == 7,
                       deps=[wtok, fm_done])
                wpart_rtok[0] = bk.wtok
                if tgt == "v":
                    evt = vop(lambda e, t=t, bk=bk: e.tensor_copy(
                        V_TM[:, t, :], bk.f32(0, 512)), deps=[bk.wtok])
                    bk.rtok = evt
                    last_tok = evt
                    continue
                td = []
                if tbl_rtok[t % 2] is not None:
                    td.append(tbl_rtok[t % 2])
                tca = p.dma("sync", tblbuf[t % 2][:, 0, :],
                            rot_d[tbls[0]][t * 128:(t + 1) * 128, :], deps=td)
                tcb = p.dma("sync", tblbuf[t % 2][:, 1, :],
                            rot_d[tbls[1]][t * 128:(t + 1) * 128, :], deps=td)
                ev = vop(lambda e, bk=bk: e.tensor_copy(q_raw, bk.f32(0, 512)),
                         deps=[bk.wtok])
                bk.rtok = ev
                sbase = 16 + 16 * (t % 2)
                rbase = 24 + 16 * (t % 2)
                s1 = vop(lambda e: e.tensor_mul(sq_b, q_raw, q_raw), deps=[ev])
                s2 = vop(lambda e, sb_=sbase: e.tensor_reduce(
                    smv[:, sb_:sb_ + 8], ar.view(SCR.off + 16384, BF16, 8, 64),
                    axis=AX.X, op=ALU.add), deps=[s1])
                s4 = aop(lambda e, sb_=sbase, rb_=rbase: e.activation(
                    smv[:, rb_:rb_ + 8], smv[:, sb_:sb_ + 8], AF.Ln, bias=TINYC),
                    deps=[s2])
                if tgt == "q":
                    s5 = aop(lambda e, rb_=rbase: e.activation(
                        smv[:, rb_:rb_ + 8], smv[:, rb_:rb_ + 8], AF.Exp, scale=-0.5,
                        bias=LN8C), deps=[s4])
                else:
                    s5 = aop(lambda e, rb_=rbase: e.activation(
                        smv[:, rb_:rb_ + 8], smv[:, rb_:rb_ + 8], AF.Exp, scale=-0.5),
                        deps=[s4])
                r1 = vop(lambda e: e.tensor_mul(q1b, q_raw, qs_bf), deps=[ev, cast_t])
                qs_rtok[0] = r1
                r2d = [r1, tca]
                if tk.get("pad_war"):
                    r2d.append(tk["pad_war"])
                r2 = vop(lambda e, t=t: e.tensor_mul(t1b, q1b, tblbuf[t % 2][:, 0, :]),
                         deps=r2d)
                shuf = ar.view(SCR.off + 17408, BF16, 8, 2, 32)[:, :, ::-1, :]
                r3 = vop(lambda e, t=t: e.tensor_tensor(
                    ar.view(SCR.off + 19456, BF16, 8, 2, 32),
                    shuf, tblbuf[t % 2][:, 1, :].rearrange("pp (h two d) -> pp h two d",
                                                           two=2, d=32),
                    op=ALU.mult), deps=[r1, tcb])
                r4 = vop(lambda e: e.tensor_tensor(t1b, t1b, t2b_, op=ALU.add),
                         deps=[r2, r3])
                qfd = [r4, s5]
                if qf_rtok[t % 2] is not None:
                    qfd.append(qf_rtok[t % 2])
                r5 = vop(lambda e, t=t, rb_=rbase: e.tensor_tensor(
                    ar.view(SCR.off + 20480 + (t % 2) * 1024, BF16, 8, 64),
                    ar.view(SCR.off + 18432, BF16, 8, 64),
                    smv[:, rb_:rb_ + 8][:, :, None].broadcast_to([128, 8, 64]),
                    op=ALU.mult), deps=qfd)
                tbl_rtok[t % 2] = r3
                if t == 0 and tgt == "q" and s == 0 and l == 0 and "dbg_qft0" in dbg_out:
                    dbg_dump("dbg_qraw0", q_raw, [ev])
                    dbg_dump("dbg_rstd0", smv[:, rbase:rbase + 8], [s5])
                    dbg_dump("dbg_q10", q1b, [r1])
                    dbg_dump("dbg_t10", t1b, [r4])
                    dbg_dump("dbg_t20", t2b_, [r3])
                    dbg_dump("dbg_qft0", ar.view(SCR.off + 20480, BF16, 512), [r5])
                DST = QFM if tgt == "q" else KFM
                ltt = None
                for c in range(4):
                    bk2 = banks[2 + ((t + c) % 2)]
                    ltt = transpose(bk2, qf_pp[:, t % 2, c * 128:(c + 1) * 128],
                                    deps=[r5])
                    last_tok = evac(bk2, DST[:, c, t * 128:(t + 1) * 128])
                qf_rtok[t % 2] = ltt
                if paranoid:
                    p.barrier()
                if tgt == "q" and s == 0 and l == 0 and "dbg_qfm_t0a" in dbg_out:
                    if t == 0:
                        p.dma("gpsimd", dbg_out["dbg_qfm_t0a"],
                              tmv(0, BF16, 4, T)[:, :, 0:128], deps=[last_tok])
                    if t == 2:
                        p.dma("gpsimd", dbg_out["dbg_qfm_t0b"],
                              tmv(0, BF16, 4, T)[:, :, 0:128], deps=[last_tok])
            if tgt == "q" and s == 0 and l == 0 and "dbg_qfm_early" in dbg_out:
                dbg_dump("dbg_qfm_early", ar.view(TM16.off, BF16, 4 * T), [last_tok])
        qkv_done = last_tok
        tk["fmb_reader"] = qkv_done
        if s == 0 and l == 0:
            for nm_, src_ in (("dbg_qfm", ar.view(TM16.off, BF16, 4 * T)),
                              ("dbg_kfm", ar.view(TM16.off + 16384, BF16, 4 * T)),
                              ("dbg_vtm", ar.view(RD.off, BF16, NT * 512))):
                if nm_ in dbg_out:
                    dbg_dump(nm_, src_, [qkv_done])

        O_FM = rdv(16384, BF16, 4, T)
        P_buf = [rdv(32768, BF16, 640), rdv(34048, BF16, 640)]
        PT_buf = [rdv(35328, BF16, 640), rdv(36608, BF16, 640)]
        qfpad = [rdv(37888, BF16, 128), rdv(38144, BF16, 128)]
        zq = vop(lambda e: e.memset(rdv(37888, BF16, 256), 0.0), deps=[qkv_done])
        qf_guard = [zq, zq]
        amlo = cv("amlo", F32, 128)
        amhi = cv("amhi", F32, 128)
        o_ev_tok = None
        p_guard = [None, None]
        pt_guard = [None, None]
        for hp in range(4):
            for it in range(NT):
                j0c = max(it - 4, 0)
                nch = it - j0c + 1
                jn = 128 * nch
                n1 = min(jn, 512)
                for X in range(2):
                    par = X * 64
                    head = 2 * hp + X
                    sb0 = banks[0 + 2 * X]
                    sb1 = banks[1 + 2 * X]
                    cpq = vop(lambda e, X=X, par=par, hp=hp, it=it: e.tensor_copy(
                        qfpad[X][par:par + 64, :],
                        QFM[par:par + 64, hp, it * 128:(it + 1) * 128]),
                        deps=[qkv_done, qf_guard[X]])
                    lhsT = qfpad[X]
                    mm(sb0, 0, n1, lhsT,
                       KFM[:, hp, j0c * 128:j0c * 128 + n1],
                       True, True, deps=[qkv_done, cpq])
                    if jn > 512:
                        qf_guard[X] = mm(sb1, 0, jn - 512, lhsT,
                                         KFM[:, hp, j0c * 128 + 512:j0c * 128 + jn],
                                         True, True, deps=[qkv_done, cpq])
                    else:
                        qf_guard[X] = sb0.wtok
                    mtoks = []
                    if it >= 4:
                        mk = vop(lambda e, sb0=sb0: e.tensor_tensor(
                            sb0.f32(0, 128), sb0.f32(0, 128), amlo, op=ALU.add),
                            deps=[sb0.wtok])
                        sb0.rtok = mk
                        sb0.wtok = mk
                        mtoks.append(mk)
                    if jn <= 512:
                        mk = vop(lambda e, sb0=sb0, jn=jn: e.tensor_tensor(
                            sb0.f32(jn - 128, jn), sb0.f32(jn - 128, jn), amhi,
                            op=ALU.add), deps=[sb0.wtok])
                        sb0.rtok = mk
                        sb0.wtok = mk
                        mtoks.append(mk)
                    else:
                        mk = vop(lambda e, sb1=sb1, jn=jn: e.tensor_tensor(
                            sb1.f32(jn - 640, jn - 512), sb1.f32(jn - 640, jn - 512),
                            amhi, op=ALU.add), deps=[sb1.wtok])
                        sb1.rtok = mk
                        sb1.wtok = mk
                        mtoks.append(mk)
                    sl = 16 + 8 * X + 16 * (it % 2)
                    x1 = vop(lambda e, sb0=sb0, n1=n1, sl=sl: e.reduce_max(
                        smv[:, sl:sl + 1], sb0.f32(0, n1), axis=AX.X),
                        deps=[sb0.wtok] + mtoks)
                    sb0.rtok = x1
                    mx = x1
                    if jn > 512:
                        x2 = vop(lambda e, sb1=sb1, jn=jn, sl=sl: e.reduce_max(
                            smv[:, sl + 1:sl + 2], sb1.f32(0, jn - 512), axis=AX.X),
                            deps=[sb1.wtok] + mtoks)
                        sb1.rtok = x2
                        mx = vop(lambda e, sl=sl: e.tensor_max(
                            smv[:, sl:sl + 1], smv[:, sl:sl + 1], smv[:, sl + 1:sl + 2]),
                            deps=[x1, x2])
                    ng = vop(lambda e, sl=sl: e.tensor_scalar_mul(
                        smv[:, sl + 2:sl + 3], smv[:, sl:sl + 1], -1.0), deps=[mx])
                    e1d = [ng]
                    if p_guard[X] is not None:
                        e1d.append(p_guard[X])
                    e1 = aop(lambda e, X=X, sb0=sb0, n1=n1, sl=sl: e.activation(
                        P_buf[X][:, 0:n1], sb0.f32(0, n1), AF.Exp,
                        bias=smv[:, sl + 2:sl + 3]), deps=e1d)
                    sb0.rtok = e1
                    lastexp = e1
                    if jn > 512:
                        e2 = aop(lambda e, X=X, sb1=sb1, jn=jn, sl=sl: e.activation(
                            P_buf[X][:, 512:jn], sb1.f32(0, jn - 512), AF.Exp,
                            bias=smv[:, sl + 2:sl + 3]), deps=[e1])
                        sb1.rtok = e2
                        lastexp = e2
                    sumtok = vop(lambda e, X=X, jn=jn, sl=sl: e.reduce_sum(
                        smv[:, sl + 3:sl + 4], P_buf[X][:, 0:jn], axis=AX.X),
                        deps=[lastexp])
                    rc = vop(lambda e, sl=sl: e.reciprocal(
                        smv[:, sl + 5:sl + 6], smv[:, sl + 3:sl + 4]), deps=[sumtok])
                    ps = vop(lambda e, X=X, jn=jn, sl=sl: e.tensor_scalar_mul(
                        P_buf[X][:, 0:jn], P_buf[X][:, 0:jn], smv[:, sl + 5:sl + 6]),
                        deps=[rc])
                    pt_toks = []
                    ltr = None
                    for c in range(nch):
                        bk = banks[4]
                        evd = [ps]
                        if pt_guard[X] is not None:
                            evd.append(pt_guard[X])
                        ltr = transpose(bk, P_buf[X][:, c * 128:(c + 1) * 128], deps=evd)
                        pt_toks.append(evac(bk, PT_buf[X][:, c * 128:(c + 1) * 128]))
                    p_guard[X] = ltr
                    for c in range(nch):
                        startf = (it % 4 == 0) and (c == 0)
                        stopf = (it % 4 == 3) and (c == nch - 1)
                        out_ap = banks[5].t.ap()[par:par + 64,
                                                 (it % 4) * 128:(it % 4) * 128 + 128]
                        avt = mm_ap(banks[5], out_ap,
                              V_TM[:, j0c + c, head * 64:(head + 1) * 64],
                              PT_buf[X][:, c * 128:(c + 1) * 128],
                              startf, stopf, deps=[pt_toks[c], qkv_done],
                              tile_position=(0, par), inc=True,
                              skip_group_check=(X == 1))
                        pt_guard[X] = avt
                if paranoid:
                    p.barrier()
                if it % 4 == 3:
                    o_ev_tok = vop(lambda e, hp=hp, it=it: e.tensor_copy(
                        O_FM[:, hp, (it - 3) * 128:(it + 1) * 128],
                        banks[5].f32(0, 512)), deps=[banks[5].wtok])
                    banks[5].rtok = o_ev_tok
        attn_o_done = o_ev_tok
        if "dbg_ofm" in dbg_out and s == 0 and l == 0:
            dbg_dump("dbg_ofm", ar.view(RD.off + 16384, BF16, 4 * T), [attn_o_done])

        fwo_sb = sv(0, BF16, 4, 1024)
        fw = p.dma("sync", fwo_sb, fwo_scr[l],
                   deps=([wpart_rtok[0]] if wpart_rtok[0] else []) +
                        [cast_tok[f"fwo{l}"]])
        res_tok = None
        for t in range(NT):
            for c in range(4):
                lhsT = O_FM[:, c, t * 128:(t + 1) * 128]
                mm(banks[6], 0, 512, lhsT, fwo_sb[:, c, 0:512], c == 0, c == 3,
                   deps=[fw, attn_o_done])
                mm(banks[7], 0, 512, lhsT, fwo_sb[:, c, 512:1024], c == 0, c == 3,
                   deps=[fw, attn_o_done])
            a1 = vop(lambda e, t=t: e.tensor_tensor(
                XS.ap()[:, t, 0:512], XS.ap()[:, t, 0:512], banks[6].f32(0, 512),
                op=ALU.add), deps=[banks[6].wtok, xs_tok])
            banks[6].rtok = a1
            a2 = vop(lambda e, t=t: e.tensor_tensor(
                XS.ap()[:, t, 512:1024], XS.ap()[:, t, 512:1024], banks[7].f32(0, 512),
                op=ALU.add), deps=[banks[7].wtok, a1])
            banks[7].rtok = a2
            res_tok = a2
            if paranoid:
                p.barrier()

        rstd_bc = rdv(0, F32, 2048)
        mrstd_bc = rdv(8192, F32, 2048)
        sq_scr = [sv(17408, BF16, 512), sv(18432, BF16, 512)]
        svv = sv(15360, F32, 512)
        h2v = tmv(0, BF16, 8, T)
        t1h = sv(19456, BF16, 512)
        h2_toks = []
        st = {"bcd": None, "prev_sv": None, "prevh": None}

        def stats_h2_block(b, ftok):
            bA = banks[(b % 2) * 2]
            bB = banks[(b % 2) * 2 + 1]
            sq_rd = None
            for c in range(8):
                off = (1 if c < 4 else 0) + b * 512
                mm(bA, 0, 512, ONES_MAT, FMB.ap()[:, c, off:off + 512],
                   c == 0, c == 7, deps=[ftok])
                dd = [ftok]
                if sq_rd is not None:
                    dd.append(sq_rd)
                sqt = vop(lambda e, c=c, off=off, b=b: e.tensor_mul(
                    sq_scr[c % 2], FMB.ap()[:, c, off:off + 512],
                    FMB.ap()[:, c, off:off + 512]), deps=dd)
                sq_rd = mm(bB, 0, 512, ONES_MAT, sq_scr[c % 2],
                           c == 0, c == 7, deps=[sqt], inc=True)
            mrs = mrstd_bc[:, b * 512:(b + 1) * 512]
            rss = rstd_bc[:, b * 512:(b + 1) * 512]
            dd = [bA.wtok, bB.wtok]
            if st["prev_sv"] is not None:
                dd.append(st["prev_sv"])
            n1_ = vop(lambda e, mrs=mrs, bA=bA: e.tensor_copy(
                mrs, bA.f32(0, 512)), deps=dd)
            bA.rtok = n1_
            n3_ = vop(lambda e, mrs=mrs: e.tensor_mul(svv, mrs, mrs),
                      deps=dd + [n1_])
            n4_ = vop(lambda e, rss=rss, bB=bB: e.scalar_tensor_tensor(
                rss, bB.f32(0, 512), 1024.0, svv, op0=ALU.mult,
                op1=ALU.subtract), deps=[n3_])
            bB.rtok = n4_
            n5_ = aop(lambda e, rss=rss: e.activation(rss, rss, AF.Ln, bias=N2EPS),
                      deps=[n4_])
            n6_ = aop(lambda e, rss=rss: e.activation(rss, rss, AF.Exp, scale=-0.5,
                                                      bias=LNN), deps=[n5_])
            n7_ = vop(lambda e, mrs=mrs, rss=rss: e.scalar_tensor_tensor(
                mrs, mrs, 1.0 / 1024.0, rss, op0=ALU.mult, op1=ALU.mult),
                deps=[n6_])
            st["prev_sv"] = n3_
            st["bcd"] = n7_
            for c in range(8):
                off = (1 if c < 4 else 0) + b * 512
                dd = [ftok, n7_]
                if st["prevh"]:
                    dd.append(st["prevh"])
                g1_ = vop(lambda e, c=c, b=b, off=off: e.tensor_tensor(
                    t1h, FMB.ap()[:, c, off:off + 512],
                    rstd_bc[:, b * 512:(b + 1) * 512], op=ALU.mult), deps=dd)
                g2_ = vop(lambda e, c=c, b=b: e.tensor_sub(
                    t1h, t1h, mrstd_bc[:, b * 512:(b + 1) * 512]), deps=[g1_])
                g3_ = vop(lambda e, c=c, b=b: e.scalar_tensor_tensor(
                    h2v[:, c, b * 512:(b + 1) * 512], t1h, lgv[:, c:c + 1],
                    lbv[:, c:c + 1].broadcast_to([128, 512]),
                    op0=ALU.mult, op1=ALU.add), deps=[g2_, lconst])
                st["prevh"] = g3_
                h2_toks.append(g3_)

        def fm_cb(t, tile_tok):
            if t % 4 == 3:
                stats_h2_block(t // 4, tile_tok)

        dd = [res_tok]
        if tk["fmb_reader"] is not None:
            dd.append(tk["fmb_reader"])
        fm2_done = xs_fm_build(dd, block_cb=fm_cb)
        tk["fmb_reader"] = fm2_done
        bcd = st["bcd"]
        h2_done = h2_toks[-1]
        tk["h2_reader"] = None
        if "dbg_h2" in dbg_out and s == 0 and l == 0:
            dbg_dump("dbg_h2", ar.view(TM16.off, BF16, 8 * T), [h2_done])

        final_tok = None
        w1a_buf = [sv(0, BF16, 8, 128), sv(4096, BF16, 8, 128)]
        w1g_buf = [sv(2048, BF16, 8, 128), sv(6144, BF16, 8, 128)]
        w2_buf = [sv(8192, BF16, 22, 128), sv(13824, BF16, 22, 128)]
        a_sb = sv(19456, BF16, 1024)
        g_sb = sv(21504, BF16, 1024)
        zb_sb = sv(19456, BF16, 1024)
        w1_rtok = [None, None]
        w2_rtok = [None, None]
        zb_guard = [None]
        for w in range(NW):
            ag_done = []
            for j in range(NJ):
                nj = 128 if j < NJ - 1 else LAST_J
                dd = [res_tok]
                if w1_rtok[j % 2] is not None:
                    dd.append(w1_rtok[j % 2])
                da = p.dma("sync", w1a_buf[j % 2][:, :, 0:nj],
                           w1_scr[l][:, :, j * 128:j * 128 + nj],
                           deps=dd + [cast_tok[f"w1{l}"]])
                dg = p.dma("sync", w1g_buf[j % 2][:, :, 0:nj],
                           w1_scr[l][:, :, HID + j * 128:HID + j * 128 + nj],
                           deps=dd + [cast_tok[f"w1{l}"]])
                ba = [banks[4 * (j % 2) + 0], banks[4 * (j % 2) + 1]]
                bg = [banks[4 * (j % 2) + 2], banks[4 * (j % 2) + 3]]
                for cd in range(8):
                    rhs0 = h2v[:, cd, w * W_T:w * W_T + 512]
                    rhs1 = h2v[:, cd, w * W_T + 512:w * W_T + 1024] if W_T > 512 else None
                    la = w1a_buf[j % 2][:, cd, 0:nj]
                    lg_ = w1g_buf[j % 2][:, cd, 0:nj]
                    mm_ap(ba[0], ba[0].t.ap()[0:nj, 0:512], la, rhs0,
                          cd == 0, cd == 7, deps=[da, h2_done], inc=(cd == 7))
                    mm_ap(bg[0], bg[0].t.ap()[0:nj, 0:512], lg_, rhs0,
                          cd == 0, cd == 7, deps=[dg, h2_done], inc=(cd == 7))
                    if rhs1 is not None:
                        mm_ap(ba[1], ba[1].t.ap()[0:nj, 0:512], la, rhs1,
                              cd == 0, cd == 7, deps=[da, h2_done], inc=(cd == 7))
                        mm_ap(bg[1], bg[1].t.ap()[0:nj, 0:512], lg_, rhs1,
                              cd == 0, cd == 7, deps=[dg, h2_done], inc=(cd == 7))
                nwb = 2 if W_T > 512 else 1
                etoks = []
                for q_ in range(nwb):
                    ead = [ba[q_].wtok, lconst]
                    if zb_guard[0] is not None:
                        ead.append(zb_guard[0])
                    ea = aop(lambda e, j=j, q_=q_, nj=nj, ba=ba: e.activation(
                        a_sb[0:nj, q_ * 512:(q_ + 1) * 512], ba[q_].t.ap()[0:nj, 0:512],
                        AF.Identity, bias=b1av[0:nj, j:j + 1]), deps=ead)
                    ba[q_].rtok = ea
                    eg = aop(lambda e, j=j, q_=q_, nj=nj, bg=bg: e.activation(
                        g_sb[0:nj, q_ * 512:(q_ + 1) * 512], bg[q_].t.ap()[0:nj, 0:512],
                        AF.Gelu, bias=b1gv[0:nj, j:j + 1]), deps=[bg[q_].wtok, lconst])
                    bg[q_].rtok = eg
                    etoks += [ea, eg]
                agt = vop(lambda e, j=j, nj=nj: e.tensor_mul(
                    rdv(j * 2048, BF16, 1024)[0:nj, 0:W_T],
                    a_sb[0:nj, 0:W_T], g_sb[0:nj, 0:W_T]), deps=etoks)
                ag_done.append(agt)
                w1_rtok[j % 2] = agt
                if paranoid:
                    p.barrier()
            tk["wpart_free"] = ag_done[-1]
            for fc in range(8):
                dd = [ag_done[0]]
                if w2_rtok[fc % 2] is not None:
                    dd.append(w2_rtok[fc % 2])
                dw0 = p.dma("sync", w2_buf[fc % 2][:],
                            w2_scr[l][:, :, fc * 128:(fc + 1) * 128],
                            deps=dd + [cast_tok[f"w2{l}"]])
                dw1 = dw0
                zb = [banks[2 * (fc % 2)], banks[2 * (fc % 2) + 1]]
                for j in range(NJ):
                    nj = 128 if j < NJ - 1 else LAST_J
                    lhsT = w2_buf[fc % 2][0:nj, j, :]
                    rhs = rdv(j * 2048, BF16, 1024)[0:nj, 0:W_T]
                    mm(zb[0], 0, 512, lhsT, rhs[:, 0:512], j == 0, j == NJ - 1,
                       deps=[dw0, dw1, ag_done[j]])
                    if W_T > 512:
                        mm(zb[1], 0, 512, lhsT, rhs[:, 512:1024], j == 0, j == NJ - 1,
                           deps=[dw0, dw1, ag_done[j]])
                w2_rtok[fc % 2] = zb[0].wtok
                nwb = 2 if W_T > 512 else 1
                ztoks = []
                for q_ in range(nwb):
                    zt = aop(lambda e, q_=q_, fc=fc, zb=zb: e.activation(
                        zb_sb[:, q_ * 512:(q_ + 1) * 512], zb[q_].f32(0, 512),
                        AF.Identity, bias=b2v[:, fc:fc + 1]),
                        deps=[zb[q_].wtok, lconst, ag_done[-1]])
                    zb[q_].rtok = zt
                    ztoks.append(zt)
                for ttl in range(W_T // 128):
                    bk = banks[4 + ttl % 2]
                    zb_guard[0] = transpose(bk, zb_sb[:, ttl * 128:(ttl + 1) * 128],
                                            deps=ztoks)
                    tglob = w * (W_T // 128) + ttl
                    rtok = vop(lambda e, tglob=tglob, fc=fc, bk=bk: e.tensor_tensor(
                        XS.ap()[:, tglob, fc * 128:(fc + 1) * 128],
                        XS.ap()[:, tglob, fc * 128:(fc + 1) * 128],
                        bk.t.ap()[:, 0:64].bitcast(BF16), op=ALU.add),
                        deps=[bk.wtok, res_tok])
                    bk.rtok = rtok
                    final_tok = rtok
                if paranoid:
                    p.barrier()
        if "dbg_xsfin" in dbg_out and s == 0 and l == 1:
            tok_ = p.dma("sync", dbg_out["dbg_xsfin"].rearrange("(nt pp) d -> pp nt d", pp=128),
                         XS.ap(), deps=[final_tok])
            for eng_ in ENGS:
                p.op(eng_, lambda e: None, deps=[tok_], inc=False)
        return final_tok

    def early_out(s, dep):
        toks = []
        for t in range(NT):
            toks.append(p.dma("sync",
                              y_out[s].rearrange("(nt p) d -> p nt d", p=128)[:, t, :],
                              XO.ap()[:, t, :], deps=[dep]))
        return toks

    def run_seq(s, prev):
        fmz = vop(lambda e: e.memset(FMB.ap(), 0.0), deps=prev)
        xview = x_in[s].rearrange("(nt p) d -> p nt d", p=128)
        xo_tile = []
        for t in range(NT):
            xo_tile.append(p.dma("sync", XO.ap()[:, t, :], xview[:, t, :],
                                 deps=[ctok] + list(prev)))
        xo_tok = xo_tile[-1]

        if stop_phase <= 0:
            return early_out(s, xo_tok)
        stage = tmv(0, BF16, 8, 1024)
        wt_dst = {"q": rdv(0, BF16, 8, 1024), "k": rdv(16384, BF16, 8, 1024),
                  "v": sv(0, BF16, 8, 1024)}
        wt_tok = {}
        stage_rtoks = [None]
        for nm, wd in (("k", wk_d), ("q", wq_d), ("v", wv_d)):
            dep = [ctok] + list(prev) + \
                ([t for t in stage_rtoks if t] if stage_rtoks else [])
            if s > 0:
                wt_tok[nm] = p.dma("sync", wt_dst[nm], wtT_scr[nm],
                                   deps=[t for t in dep if t] +
                                        [tk["wtT_" + nm]])
                continue
            stok = p.dma("gpsimd", stage,
                         wd.rearrange("(cf p) d -> p cf d", p=128),
                         deps=[t for t in dep if t])
            stage_rtoks = []
            etoks = []
            for cf in range(8):
                for cd in range(8):
                    bk = banks[6 + (cd % 2)]
                    tt = transpose(bk, stage[:, cf, cd * 128:(cd + 1) * 128],
                                   deps=[stok])
                    et = evac(bk, wt_dst[nm][:, cd, cf * 128:(cf + 1) * 128])
                    etoks.append(et)
                    stage_rtoks = [tt]
            wt_tok[nm] = etoks[-1]
            tk["wtT_" + nm] = p.dma("gpsimd", wtT_scr[nm], wt_dst[nm],
                                    deps=[etoks[-1]])
            if paranoid:
                p.barrier()
            if nm == "q" and "dbg_wqT_early" in dbg_out and s == 0:
                dbg_dump("dbg_wqT_early", rdv(0, BF16, 8 * 1024), [etoks[-1]])
            if nm == "q" and "dbg_idn" in dbg_out and s == 0:
                dbg_dump("dbg_idn", IDN, [etoks[-1]])

        if s == 0:
            cast_tok["wqkv0"] = p.dma("gpsimd", wqkv_scr[0],
                wqkv_d[0].rearrange("(cd pp) f -> pp cd f", pp=128))
            cast_tok["fwo0"] = p.dma("gpsimd", fwo_scr[0],
                fwo_d[0].rearrange("(c pp) f -> pp c f", pp=128))
            cast_tok["w10"] = p.dma("gpsimd", w1_scr[0],
                w1_d[0].rearrange("(cd pp) f -> pp cd f", pp=128))
            p.dma("gpsimd", w2_scr[0, :, 0:21, :],
                  w2_d[0, 0:21 * 128, :].rearrange("(cj pp) f -> pp cj f", pp=128))
            cast_tok["w20"] = p.dma("gpsimd", w2_scr[0, 0:LAST_J, 21, :],
                                    w2_d[0, 21 * 128:HID, :])
            cast_tok["wqkv1"] = p.dma("gpsimd", wqkv_scr[1],
                wqkv_d[1].rearrange("(cd pp) f -> pp cd f", pp=128))
            cast_tok["fwo1"] = p.dma("gpsimd", fwo_scr[1],
                fwo_d[1].rearrange("(c pp) f -> pp c f", pp=128))
            cast_tok["w11"] = p.dma("gpsimd", w1_scr[1],
                w1_d[1].rearrange("(cd pp) f -> pp cd f", pp=128))
            p.dma("gpsimd", w2_scr[1, :, 0:21, :],
                  w2_d[1, 0:21 * 128, :].rearrange("(cj pp) f -> pp cj f", pp=128))
            cast_tok["w21"] = p.dma("gpsimd", w2_scr[1, 0:LAST_J, 21, :],
                                    w2_d[1, 21 * 128:HID, :])
            cast_tok["wo"] = p.dma("gpsimd", wattn_scr["o"],
                wo_d.rearrange("(cf p) d -> p cf d", p=128))
            zb = ar.view(SCR.off + 18432, BF16, 1024)
            zt = vop(lambda e: e.memset(zb, 0.0))
            p.dma("gpsimd", w2_scr[0, LAST_J:128, 21, :], zb[LAST_J:128, :],
                  deps=[zt])
            tk["pad_war"] = p.dma("gpsimd", w2_scr[1, LAST_J:128, 21, :],
                                  zb[LAST_J:128, :], deps=[zt])

        if stop_phase <= 1:
            return early_out(s, wt_tok["v"])
        bounce = sv(16384, BF16, 2, 1024)
        fm_toks = []
        bounce_rtok = [None, None]
        ln1_war = [None, None]
        for t in range(NT):
            xt = XO.ap()[:, t, :]
            stok, rstd, negmr = ln_stats2([xt[:, 0:512], xt[:, 512:1024]], banks[4],
                                          0 if t % 2 == 0 else 8,
                                          deps=[xo_tile[t]],
                                          war_dep=ln1_war[t % 2])
            bb = bounce[:, t % 2, :]
            dd = [stok]
            if bounce_rtok[t % 2] is not None:
                dd.append(bounce_rtok[t % 2])
            btok = vop(lambda e, bb=bb, r=rstd, n=negmr, t=t:
                       e.tensor_scalar(bb, XO.ap()[:, t, :], r, n,
                                       op0=ALU.mult, op1=ALU.add),
                       deps=dd)
            ln1_war[t % 2] = btok
            if t == 0 and "dbg_smalls" in dbg_out and s == 0:
                dbg_dump("dbg_smalls", cv("smalls", F32, 64), [stok])
                dbg_dump("dbg_bounce", bb, [btok])
            last_t = None
            for c in range(8):
                bk = banks[6 + (c % 2)]
                tt = transpose(bk, bb[:, c * 128:(c + 1) * 128], deps=[btok])
                et = evac(bk, FMB.ap()[:, c, 1 + t * 128: 1 + (t + 1) * 128])
                fm_toks.append(et)
                last_t = tt
            bounce_rtok[t % 2] = last_t
            if paranoid:
                p.barrier()
        fm_done = fm_toks[-1]
        if "dbg_lnxfm" in dbg_out and s == 0:
            dbg_dump("dbg_lnxfm", ar.view(FMB.off, BF16, 8 * FMW), [fm_done])

        if stop_phase <= 2:
            return early_out(s, fm_done)
        q_tm = rdv(32768, BF16, 16, 64)
        k_tm = rdv(32768 + 2048, BF16, 16, 64)
        v_tm = rdv(32768 + 4096, BF16, 16, 64)
        qfm_c = rdv(32768 + 6144, BF16, 8, 128)
        kpad = rdv(32768 + 8192, BF16, 16, 128)
        a_sb = rdv(32768 + 12288, BF16, 16, 128)
        s_cur = sv(20480, F32, 8, 64)
        s_tmp = sv(22528, F32, 8, 64)
        s_b16 = sv(16384, BF16, 16, 64)
        gq = cv("gq", F32, 16)
        gk = cv("gk", F32, 16)
        gC1 = cv("gC1", F32, 8)
        gC = cv("gC", F32, 8)
        g1 = cv("g1", F32, 8)
        maskk = cv("maskk", F32, 128)

        st_init = vop(lambda e: e.memset(s_cur, 0.0))
        kp_init = vop(lambda e: e.memset(kpad, 0.0))
        sb_zero = vop(lambda e: e.memset(s_b16, 0.0))
        banks[4].rtok = banks[4].wtok
        sb_init = None

        prev_state_tok = st_init
        prev_sb_tok = None
        y_toks = []
        for t in range(NT):
            deps_w = [wt_tok["q"], wt_tok["k"], wt_tok["v"], fm_done]
            for cd in range(8):
                lhsT = FMB.ap()[:, cd, 1 + t * 128: 1 + (t + 1) * 128]
                first = (cd == 0)
                last = (cd == 7)
                mm(banks[0], 0, 512, lhsT, wt_dst["q"][:, cd, 0:512], first, last, deps=deps_w)
                mm(banks[1], 0, 512, lhsT, wt_dst["q"][:, cd, 512:1024], first, last, deps=deps_w)
                mm(banks[2], 0, 512, lhsT, wt_dst["k"][:, cd, 0:512], first, last, deps=deps_w)
                mm(banks[3], 0, 512, lhsT, wt_dst["k"][:, cd, 512:1024], first, last, deps=deps_w)
                mm(banks[4], 0, 512, lhsT, wt_dst["v"][:, cd, 0:512], first, last, deps=deps_w)
                mm(banks[5], 0, 512, lhsT, wt_dst["v"][:, cd, 512:1024], first, last, deps=deps_w)
            eq0 = vop(lambda e: e.tensor_tensor(
                q_tm[:, 0:8, :], banks[0].f32v(0, 8, 64), bc3(gq[:, 0:8], 64), op=ALU.mult),
                deps=[banks[0].wtok])
            banks[0].rtok = eq0
            eq1 = vop(lambda e: e.tensor_tensor(
                q_tm[:, 8:16, :], banks[1].f32v(0, 8, 64), bc3(gq[:, 8:16], 64), op=ALU.mult),
                deps=[banks[1].wtok])
            banks[1].rtok = eq1
            ek0 = vop(lambda e: e.tensor_tensor(
                k_tm[:, 0:8, :], banks[2].f32v(0, 8, 64), bc3(gk[:, 0:8], 64), op=ALU.mult),
                deps=[banks[2].wtok])
            banks[2].rtok = ek0
            ek1 = vop(lambda e: e.tensor_tensor(
                k_tm[:, 8:16, :], banks[3].f32v(0, 8, 64), bc3(gk[:, 8:16], 64), op=ALU.mult),
                deps=[banks[3].wtok])
            banks[3].rtok = ek1
            ev0 = vop(lambda e: e.tensor_copy(v_tm[:, 0:8, :], banks[4].f32v(0, 8, 64)),
                      deps=[banks[4].wtok])
            banks[4].rtok = ev0
            ev1 = vop(lambda e: e.tensor_copy(v_tm[:, 8:16, :], banks[5].f32v(0, 8, 64)),
                      deps=[banks[5].wtok])
            banks[5].rtok = ev1
            qf_toks = []
            for c in range(8):
                bk = banks[6 + (c % 2)]
                tt = transpose(bk, ar.view(RD.off + 32768, BF16, 1024)[:, c * 128:(c + 1) * 128],
                               deps=[eq0, eq1])
                qf_toks.append(evac(bk, qfm_c[:, c, :]))
            kf_toks = []
            for c in range(8):
                bk = banks[6 + (c % 2)]
                tt = transpose(bk, ar.view(RD.off + 32768 + 2048, BF16, 1024)[:, c * 128:(c + 1) * 128],
                               deps=[ek0, ek1, kp_init])
                e1_ = vop(lambda e, bk=bk, c=c: e.tensor_copy(
                    kpad[0:64, 2 * c, :], bk.t.ap()[0:64, 0:64].bitcast(BF16)),
                    deps=[bk.wtok])
                e2_ = vop(lambda e, bk=bk, c=c: e.tensor_copy(
                    kpad[64:128, 2 * c + 1, :], bk.t.ap()[64:128, 0:64].bitcast(BF16)),
                    deps=[e1_])
                bk.rtok = e2_
                kf_toks.append(e2_)
            qk_done = qf_toks + kf_toks
            if retpart == "a":
                yy = vop(lambda e, t=t: e.memset(TM16.ap()[:, t, :], 0.0),
                         deps=[eq0, eq1, ek0, ek1, ev0, ev1] + qk_done)
                y_toks += [yy, yy]
                prev_state_tok = st_init
                prev_sb_tok = None
                continue
            sc_toks = {}
            for h in range(16):
                bk = banks[h // 4]
                lo = (h % 4) * 128
                sc_toks[h] = mm(bk, lo, lo + 128, kpad[:, h, :], qfm_c[:, h // 2, :],
                                True, True, deps=qk_done)
            a_toks = {}
            for b in range(4):
                bk = banks[b]
                tok = vop(lambda e, bk=bk, b=b: e.tensor_tensor(
                    a_sb[:, 4 * b:4 * b + 4, :], bk.f32v(0, 4, 128),
                    maskk[:, None, :].broadcast_to([128, 4, 128]), op=ALU.mult),
                    deps=[bk.wtok])
                bk.rtok = tok
                a_toks[b] = tok
            if retpart == "b":
                yy = vop(lambda e, t=t: e.memset(TM16.ap()[:, t, :], 0.0),
                         deps=[a_toks[0], a_toks[1], a_toks[2], a_toks[3]])
                y_toks += [yy, yy]
                continue
            for h in range(16):
                yb = banks[6 + h // 8]
                lo = (h % 8) * 64
                par = (h % 2) * 64
                d_in = [a_toks[h // 4], ev0, ev1]
                first_of_bank = (h % 8 == 0)
                last_of_bank = (h % 8 == 7)
                intra_is_last = last_of_bank and (t == 0)
                mm(yb, lo, lo + 64, a_sb[:, h, :], v_tm[:, h, :],
                   first_of_bank, intra_is_last, deps=d_in, inc=intra_is_last)
                if t > 0 and h % 2 == 1:
                    mm(yb, lo - 64, lo + 64, qfm_c[:, h // 2, :],
                       s_b16[:, h - 1:h + 1, :], False, h % 8 == 7,
                       deps=[prev_sb_tok], inc=(h % 8 == 7))
            ye0 = vop(lambda e, t=t: e.tensor_copy(TM16.ap()[:, t, 0:512],
                                                   banks[6].f32(0, 512)),
                      deps=[banks[6].wtok])
            banks[6].rtok = ye0
            ye1 = vop(lambda e, t=t: e.tensor_copy(TM16.ap()[:, t, 512:1024],
                                                   banks[7].f32(0, 512)),
                      deps=[banks[7].wtok])
            banks[7].rtok = ye1
            y_toks += [ye0, ye1]
            if retpart == "c":
                continue
            st_mm_toks = []
            for h in range(16):
                par = (h % 2) * 64
                lo = (h // 2) * 64
                out_ap = banks[4].t.ap()[par:par + 64, lo:lo + 64]
                tok = mm_ap(banks[4], out_ap, k_tm[:, h, :], v_tm[:, h, :],
                            (h <= 1), (h >= 14), deps=[ek0, ek1, ev0, ev1],
                            tile_position=(0, par), inc=(h == 15),
                            skip_group_check=(h % 2 == 1))
            st_wtok = banks[4].wtok
            e1 = vop(lambda e: e.tensor_tensor(
                s_tmp, banks[4].f32v(0, 8, 64), bc3(gC1, 64), op=ALU.mult),
                deps=[st_wtok, prev_state_tok])
            banks[4].rtok = e1
            e2 = vop(lambda e: e.tensor_tensor(
                s_cur, s_cur, bc3(gC, 64), op=ALU.mult), deps=[e1, prev_state_tok])
            e3 = vop(lambda e: e.tensor_tensor(s_cur, s_cur, s_tmp, op=ALU.add),
                     deps=[e2])
            e4a = vop(lambda e: e.tensor_tensor(
                s_b16[0:64, 0:16:2, :], s_cur[0:64, :, :],
                g1[0:64, :, None].broadcast_to([64, 8, 64]), op=ALU.mult),
                deps=[e3, sb_zero])
            e4 = vop(lambda e: e.tensor_tensor(
                s_b16[64:128, 1:16:2, :], s_cur[64:128, :, :],
                g1[64:128, :, None].broadcast_to([64, 8, 64]), op=ALU.mult),
                deps=[e4a])
            prev_state_tok = e3
            prev_sb_tok = e4
            if paranoid:
                p.barrier()
        y_done = y_toks[-1]
        tk["wpart_free"] = y_done
        for nm, src in (("dbg_wqT", rdv(0, BF16, 8 * 1024)),
                        ("dbg_wkT", rdv(16384, BF16, 8 * 1024)),
                        ("dbg_qtm_r", rdv(32768, BF16, 1024)),
                        ("dbg_ktm_r", rdv(32768 + 2048, BF16, 1024)),
                        ("dbg_vtm_r", rdv(32768 + 4096, BF16, 1024)),
                        ("dbg_asb", rdv(32768 + 10240, BF16, 16 * 128)),
                        ("dbg_qfmc", rdv(32768 + 6144, BF16, 1024)),
                        ("dbg_kfmc", rdv(32768 + 8192, BF16, 1024))):
            if nm in dbg_out and s == 0:
                dbg_dump(nm, src, [y_done])
        if "dbg_ytm" in dbg_out and s == 0:
            dbg_dump("dbg_ytm", ar.view(TM16.off, BF16, NT * 1024), [y_done])

        if stop_phase <= 3:
            return early_out(s, y_done)
        yfm_toks = []
        for c in range(8):
            for t in range(NT):
                bk = banks[(t % 2)]
                tt = transpose(bk, TM16.ap()[:, t, c * 128:(c + 1) * 128],
                               deps=y_toks[2 * t:2 * t + 2])
                yfm_toks.append(evac(bk, FMB.ap()[:, c, t * 128:(t + 1) * 128]))
        yfm_done = yfm_toks[-1]

        def YFM(c):
            return FMB.ap()[:, c, 0:T]

        gn_toks = []
        scrb = banks[2]
        nblk = T // 512
        for c in range(8):
            dep = [yfm_done]
            if scrb.rtok is not None:
                dep.append(scrb.rtok)
            last = dep[0]
            for b in range(nblk):
                a1 = vop(lambda e, c=c, b=b: e.reduce_sum(
                    sm(32 + b), YFM(c)[:, b * 512:(b + 1) * 512], axis=AX.X), deps=dep)
                a2a = aop(lambda e, c=c, b=b: e.activation(
                    scrb.f32(0, 512), YFM(c)[:, b * 512:(b + 1) * 512], AF.Square),
                    deps=[a1])
                a2 = vop(lambda e, b=b: e.reduce_sum(
                    sm(36 + b), scrb.f32(0, 512), axis=AX.X), deps=[a2a])
                dep = [a2]
                last = a2
            scrb.wtok = last
            scrb.rtok = last
            for (dst, src0) in ((16 + c, 32), (24 + c, 36)):
                tkg = vop(lambda e, dst=dst, src0=src0: e.tensor_copy(sm(dst), sm(src0)),
                         deps=[last])
                for b in range(1, nblk):
                    tkg = vop(lambda e, dst=dst, s2=src0 + b: e.tensor_tensor(
                        sm(dst), sm(dst), sm(s2), op=ALU.add), deps=[tkg])
                gn_toks.append(tkg)
        halfsel = cv("halfsel", F32, 2)
        smrow = cv("smalls", F32, 64)
        rs_sq = smrow[:, 16:32]
        hm0 = mm_ap(banks[5], banks[5].t.ap()[0:1, 0:16], halfsel[:, 0:1], rs_sq,
                    True, True, deps=gn_toks)
        hm1 = mm_ap(banks[5], banks[5].t.ap()[0:1, 16:32], halfsel[:, 1:2], rs_sq,
                    True, True, deps=gn_toks)
        banks[5].wtok = hm1
        ngrp = float(T * 64)
        b5 = banks[5].t.ap()
        bc_toks = []
        for h in range(2):
            me, tv, t2, rs, mr = 8 * h, 16 + 8 * h, 24 + 8 * h, 32 + 8 * h, 48 + 8 * h
            k1 = vop(lambda e, h=h, me=me: e.tensor_scalar_mul(
                smrow[0:1, me:me + 8], b5[0:1, 16 * h:16 * h + 8], 1.0 / ngrp),
                deps=[hm1])
            k2 = vop(lambda e, h=h, tv=tv: e.tensor_scalar_mul(
                smrow[0:1, tv:tv + 8], b5[0:1, 16 * h + 8:16 * h + 16], 1.0 / ngrp),
                deps=[k1])
            banks[5].rtok = k2
            k3 = vop(lambda e, me=me, t2=t2: e.tensor_mul(
                smrow[0:1, t2:t2 + 8], smrow[0:1, me:me + 8], smrow[0:1, me:me + 8]),
                deps=[k2])
            k4 = vop(lambda e, tv=tv, t2=t2: e.tensor_sub(
                smrow[0:1, tv:tv + 8], smrow[0:1, tv:tv + 8], smrow[0:1, t2:t2 + 8]),
                deps=[k3])
            k5 = aop(lambda e, tv=tv, rs=rs: e.activation(
                smrow[0:1, rs:rs + 8], smrow[0:1, tv:tv + 8], AF.Ln, bias=EPSC[0:1, :]),
                deps=[k4])
            k6 = aop(lambda e, rs=rs: e.activation(
                smrow[0:1, rs:rs + 8], smrow[0:1, rs:rs + 8], AF.Exp, scale=-0.5),
                deps=[k5])
            k7 = vop(lambda e, me=me, rs=rs, mr=mr: e.tensor_mul(
                smrow[0:1, mr:mr + 8], smrow[0:1, me:me + 8], smrow[0:1, rs:rs + 8]),
                deps=[k6])
            m1 = mm_ap(banks[4], banks[4].t.ap()[:, 8 * h:8 * h + 8], ONES_F32,
                       smrow[0:1, rs:rs + 8], h == 0, False, deps=[k7])
            m2 = mm_ap(banks[4], banks[4].t.ap()[:, 16 + 8 * h:24 + 8 * h], ONES_F32,
                       smrow[0:1, mr:mr + 8], False, h == 1, deps=[k7],
                       inc=(h == 1))
            bc_toks.append(m2)
        bctok = bc_toks[-1]
        banks[4].wtok = bctok
        Sf = sv(0, F32, 8)
        Bf = sv(64, F32, 8)
        TmpF = sv(128, F32, 8)
        gnw = cv("gnw", F32, 8)
        gnb = cv("gnb", F32, 8)
        b4 = banks[4].t.ap()
        f1 = vop(lambda e: e.tensor_mul(Sf[0:64, :], b4[0:64, 0:8], gnw[0:64, :]),
                 deps=[bctok])
        f2 = vop(lambda e: e.tensor_mul(Sf[64:128, :], b4[64:128, 8:16], gnw[64:128, :]),
                 deps=[f1])
        f3 = vop(lambda e: e.tensor_mul(TmpF[0:64, :], b4[0:64, 16:24], gnw[0:64, :]),
                 deps=[f2])
        f4 = vop(lambda e: e.tensor_mul(TmpF[64:128, :], b4[64:128, 24:32], gnw[64:128, :]),
                 deps=[f3])
        banks[4].rtok = f4
        f5 = vop(lambda e: e.tensor_sub(Bf, gnb, TmpF), deps=[f4])
        yn_toks = []
        for c in range(8):
            yn_toks.append(aop(lambda e, c=c: e.activation(
                YFM(c), YFM(c), AF.Identity, bias=Bf[:, c:c + 1], scale=Sf[:, c:c + 1]),
                deps=[f5] + gn_toks))
        yn_done = yn_toks[-1]
        if "dbg_ynfm" in dbg_out and s == 0:
            dbg_dump("dbg_ynfm", FMB.ap()[:, :, 0:T], [yn_done])

        woT = rdv(0, BF16, 8, 1024)
        if s > 0 and tk.get("wtT_o"):
            woT_done = p.dma("sync", woT, wtT_scr["o"],
                             deps=[yfm_done, tk["wtT_o"]])
        else:
            stage2 = tmv(0, BF16, 8, 1024)
            sw = p.dma("sync", stage2, wattn_scr["o"],
                       deps=[yfm_done, cast_tok["wo"]])
            wo_toks = []
            for cf in range(8):
                for cd in range(8):
                    bk = banks[6 + (cd % 2)]
                    transpose(bk, stage2[:, cf, cd * 128:(cd + 1) * 128], deps=[sw])
                    wo_toks.append(evac(bk, woT[:, cd, cf * 128:(cf + 1) * 128]))
            woT_done = wo_toks[-1]
            tk["wtT_o"] = p.dma("gpsimd", wtT_scr["o"], woT, deps=[woT_done])

        ln2_war = [None, None]
        attn_tile_tok = []
        for t in range(NT):
            for cd in range(8):
                lhsT = FMB.ap()[:, cd, t * 128:(t + 1) * 128]
                mm(banks[6], 0, 512, lhsT, woT[:, cd, 0:512], cd == 0, cd == 7,
                   deps=[yn_done, woT_done])
                mm(banks[7], 0, 512, lhsT, woT[:, cd, 512:1024], cd == 0, cd == 7,
                   deps=[yn_done, woT_done])
            stok, rstd, negmr = ln_stats2([banks[6].f32(0, 512), banks[7].f32(0, 512)],
                                          banks[5], 0 if t % 2 == 0 else 8,
                                          deps=[banks[6].wtok, banks[7].wtok],
                                          war_dep=ln2_war[t % 2], from_psum=True)
            r1 = vop(lambda e, t=t, n=negmr: e.tensor_scalar_add(
                XO.ap()[:, t, :], XO.ap()[:, t, :], n), deps=[stok])
            r2 = vop(lambda e, t=t, r=rstd: e.scalar_tensor_tensor(
                XO.ap()[:, t, 0:512], banks[6].f32(0, 512), r,
                XO.ap()[:, t, 0:512], op0=ALU.mult, op1=ALU.add), deps=[r1])
            banks[6].rtok = r2
            r3 = vop(lambda e, t=t, r=rstd: e.scalar_tensor_tensor(
                XO.ap()[:, t, 512:1024], banks[7].f32(0, 512), r,
                XO.ap()[:, t, 512:1024], op0=ALU.mult, op1=ALU.add), deps=[r2])
            banks[7].rtok = r3
            ln2_war[t % 2] = r3
            attn_tile_tok.append(r3)
            if paranoid:
                p.barrier()
        attn_done = r3
        if "dbg_xattn" in dbg_out:
            dbg_dump_src = XO.ap()
            tok_ = p.dma("gpsimd",
                         dbg_out["dbg_xattn"][s].rearrange("(nt p) d -> p nt d", p=128),
                         dbg_dump_src, deps=[attn_done])
            for eng_ in ENGS:
                p.op(eng_, lambda e: None, deps=[tok_], inc=False)

        if stop_phase <= 4:
            return early_out(s, attn_done)
        XS = XO
        ln3_toks = []
        ln3_war = [None, None]
        spill_view = xo_spill_d.rearrange("(nt p) d -> p nt d", p=128)
        for t in range(NT):
            xt = XS.ap()[:, t, :]
            spill_t = p.dma("gpsimd", spill_view[:, t, :], xt,
                            deps=[attn_done])
            stok, rstd, negmr = ln_stats2([xt[:, 0:512], xt[:, 512:1024]], banks[4],
                                          48 if t % 2 == 0 else 56,
                                          deps=[spill_t],
                                          war_dep=ln3_war[t % 2])
            tk3 = aop(lambda e, xt=xt, r=rstd, n=negmr: e.activation(
                xt, xt, AF.Identity, bias=n, scale=r), deps=[stok, spill_t])
            ln3_toks.append(tk3)
            ln3_war[t % 2] = tk3
        xs_tok = ln3_toks[-1]
        if "dbg_xs1" in dbg_out and s == 0:
            tok_ = p.dma("sync", dbg_out["dbg_xs1"].rearrange("(nt p) d -> p nt d", p=128),
                         XS.ap(), deps=[xs_tok])
            for eng_ in ENGS:
                p.op(eng_, lambda e: None, deps=[tok_], inc=False)

        if stop_phase <= 5:
            return early_out(s, xs_tok)
        for l in range(2):
            xs_tok = run_ffn_layer(s, l, xs_tok)
            if stop_phase <= 6 + l:
                return early_out(s, xs_tok)

        xob = sv(0, F32, 2, 1024)
        out_toks = []
        xob_rtok = [None, None]
        f2_war = [None, None]
        for t in range(NT):
            xt = XS.ap()[:, t, :]
            stok, rstd, negmr = ln_stats2([xt[:, 0:512], xt[:, 512:1024]],
                                          banks[4 + ((t + 1) % 2)],
                                          0 if t % 2 == 0 else 8, deps=[xs_tok],
                                          war_dep=f2_war[t % 2])
            dl = [xs_tok]
            if xob_rtok[t % 2] is not None:
                dl.append(xob_rtok[t % 2])
            ld = p.dma("sync", xob[:, t % 2, :],
                       xo_spill_d.rearrange("(nt p) d -> p nt d", p=128)[:, t, :],
                       deps=dl)
            if t == 0 and "dbg_xob_ld" in dbg_out and s == 0:
                dbg_dump("dbg_xob_ld", xob[:, 0, :], [ld, stok])
            o1 = vop(lambda e, t=t, r=rstd, xt=xt: e.scalar_tensor_tensor(
                xob[:, t % 2, :], xt, r, xob[:, t % 2, :],
                op0=ALU.mult, op1=ALU.add), deps=[stok, ld])
            if t == 0 and "dbg_xob_o1" in dbg_out and s == 0:
                dbg_dump("dbg_xob_o1", xob[:, 0, :], [o1])
                dbg_dump("dbg_f2rstd", cv("smalls", F32, 64)[:, 6:8], [stok])
            o2 = vop(lambda e, t=t, n=negmr: e.tensor_scalar_add(
                xob[:, t % 2, :], xob[:, t % 2, :], n), deps=[o1])
            ot = p.dma("sync", y_out[s].rearrange("(nt p) d -> p nt d", p=128)[:, t, :],
                       xob[:, t % 2, :], deps=[o2])
            xob_rtok[t % 2] = ot
            f2_war[t % 2] = o2
            out_toks.append(ot)
            if paranoid:
                p.barrier()
        return out_toks

    prev = [ctok]
    for s in range(nseq):
        if s > 0:
            p.barrier()
        prev = run_seq(s, prev)

    p.raw("sync", lambda e, v=p.cnt["dsp"], sem=p.sems["dsp"]: e.wait_ge(sem, v))
    p.raw("sync", lambda e, v=p.cnt["dact"], sem=p.sems["dact"]: e.wait_ge(sem, v))
    p.raw("sync", lambda e, v=p.cnt["dpool"], sem=p.sems["dpool"]: e.wait_ge(sem, v))

    with nc.Block() as block:
        p.finish(block)
    return nc


_NC_CACHE = {}


def kernel(**inputs):
    T = 2048
    nseq = 2
    consts = build_consts(T)
    lay = layout_inputs(inputs, T)
    key = (T, nseq)
    if key not in _NC_CACHE:
        _NC_CACHE[key] = build_core(T=T, nseq=nseq)
    nc = _NC_CACHE[key]

    x = np.ascontiguousarray(inputs["L0_attn_x"], dtype=np.float32)
    common = {}
    for nm in ("L0_attn_W_q", "L0_attn_W_k", "L0_attn_W_v", "L0_attn_W_o",
               "L0_ffn_wqkv", "L0_ffn_wo", "L0_ffn_w1", "L0_ffn_w2"):
        common[nm] = np.ascontiguousarray(inputs[nm], dtype=np.float32)
    for nm in ("identity", "ones_mat", "ones_row_f32", "gq_tbl", "gk_tbl", "gC1_tbl",
               "gC_tbl", "g1_tbl", "mask_keep", "amask_lo", "amask_hi",
               "rot_cq", "rot_sq", "rot_ck", "rot_sk", "halfsel"):
        common[nm] = consts[nm]
    for nm in ("gnw_fm", "gnb_fm", "lg_fm", "lb_fm", "b2_fm", "b1a", "b1g",
               "qs_rep", "ks_rep"):
        common[nm] = lay[nm]

    in_maps = []
    for c in range(NCORES):
        m = dict(common)
        m["x"] = x[c * nseq:(c + 1) * nseq]
        in_maps.append(m)
    res = run_bass_kernel_spmd(nc, in_maps, core_ids=list(range(NCORES)))
    out = np.concatenate([np.asarray(r["y"], dtype=np.float32) for r in res.results],
                         axis=0)
    return out
